# version 03
# baseline (speedup 1.0000x reference)
import math
from collections import defaultdict

import numpy as np
import ml_dtypes

import concourse.bass as bass
import concourse.mybir as mybir
from concourse.bass_utils import run_bass_kernel_spmd

F32 = mybir.dt.float32
BF16 = mybir.dt.bfloat16
U16 = mybir.dt.uint16
ALU = mybir.AluOpType
AFT = mybir.ActivationFunctionType
AX = mybir.AxisListType

B, S, E, H, D, M = 4, 4096, 2048, 16, 128, 16384
TQ = 2048
EC = E // 128
BFNP = np.dtype(ml_dtypes.bfloat16)
IDM = list(range(32))

ENGS = ("sync", "tensor", "vector", "scalar", "gpsimd")

PHASES = 5
NSUP = 16


class KB:

    def __init__(self, nc):
        self.nc = nc
        self.ops = {e: [] for e in ENGS}
        self.cnt = defaultdict(int)
        self.sems = {}
        self.marks = {e: [] for e in ENGS}

    def phase(self, name):
        for e in ENGS:
            self.marks[e].append((len(self.ops[e]), name))

    def sem(self, name):
        if name not in self.sems:
            self.sems[name] = self.nc.semaphore("sem_" + name).__enter__()
        return self.sems[name]

    def on(self, eng, fn):
        self.ops[eng].append(fn)

    def wait(self, eng, sem_name, val):
        if val <= 0:
            return
        s = self.sem(sem_name)
        self.on(eng, lambda e, s=s, v=val: e.wait_ge(s, v))

    def dma(self, eng, out, in_, **kw):
        name = eng + "d"
        s = self.sem(name)
        self.cnt[name] += 16
        v = self.cnt[name]
        self.on(eng, lambda e, o=out, i=in_, kw=dict(kw), s=s:
                e.dma_start(out=o, in_=i, **kw).then_inc(s, 16))
        return v

    def op(self, eng, meth, *args, **kw):
        self.on(eng, lambda e, m=meth, a=args, k=dict(kw): getattr(e, m)(*a, **k))

    def opi(self, eng, meth, *args, **kw):
        s = self.sem(eng)
        self.cnt[eng] += 1
        v = self.cnt[eng]
        self.on(eng, lambda e, m=meth, a=args, k=dict(kw), s=s:
                getattr(e, m)(*a, **k).then_inc(s, 1))
        return v

    def tiny(self, eng, meth, *args, **kw):
        v = self.opi(eng, meth, *args, **kw)
        self.wait(eng, eng, v)
        return v

    def barrier(self):
        snap = [(n, self.cnt[n]) for n in list(self.cnt) if self.cnt[n] > 0]
        for e in ENGS:
            for n, v in snap:
                if n == e:
                    continue
                self.wait(e, n, v)


def build_core_program(nc, dbg=()):
    kb = KB(nc)

    def dram(name, shape, dtype, kind="ExternalInput"):
        if kind == "Internal" and name in dbg:
            kind = "ExternalOutput"
        return nc.dram_tensor(name, list(shape), dtype, kind=kind).ap()

    x = dram("x", (S, E), F32)
    xq = dram("xq", (TQ, E), F32)
    wT = {n: dram(n, (E, E), F32) for n in ("wqT", "wkT", "wvT", "woT")}
    bvecs = {n: dram(n, (1, E), F32) for n in ("bq", "bk", "bv", "bo")}
    gpT = dram("gpT", (E // 4, M), U16)
    upT = dram("upT", (E // 4, M), U16)
    dpT = dram("dpT", (M // 4, E), U16)
    sgT = dram("sgT", (E // 32, M), F32)
    suT = dram("suT", (E // 32, M), F32)
    sdT = dram("sdT", (M // 32, E), F32)
    cos_kv = dram("cos_kv", (D, S), F32)
    sinp_kv = dram("sinp_kv", (D, S), F32)
    cos_q = dram("cos_q", (D, TQ), F32)
    sinp_q = dram("sinp_q", (D, TQ), F32)
    onesb = dram("onesb", (1, 512), BF16)
    hsel_d = dram("hsel", (128, 256), BF16)
    out = dram("out", (TQ, E), F32, kind="ExternalOutput")

    xntm = dram("xntm", (S, E), BF16, kind="Internal")
    xnqtm = dram("xnqtm", (TQ, E), BF16, kind="Internal")
    xnfm = dram("xnfm", (E, S), BF16, kind="Internal")
    xnqfm = dram("xnqfm", (E, TQ), BF16, kind="Internal")
    kcache = dram("kcache", (E, S), BF16, kind="Internal")
    vcache = dram("vcache", (S, H * 129), BF16, kind="Internal")
    qcache = dram("qcache", (E, TQ), BF16, kind="Internal")
    mtmp = dram("mtmp", (16, 1), F32, kind="Internal")
    mglob = dram("mglob", (1, 1), F32, kind="Internal")
    avsp = dram("avsp", (TQ, 2064), F32, kind="Internal")
    avn = dram("avn", (TQ, E), BF16, kind="Internal")
    x1sp = dram("x1sp", (TQ, E), F32, kind="Internal")
    xn2tm = dram("xn2tm", (TQ, E), BF16, kind="Internal")
    xn2fm = dram("xn2fm", (E, TQ), BF16, kind="Internal")
    acc = dram("acc", (TQ, E), F32, kind="Internal")
    sge = dram("sge", (E // 4, M), BF16, kind="Internal")
    sue = dram("sue", (E // 4, M), BF16, kind="Internal")
    sde = dram("sde", (M // 4, E), BF16, kind="Internal")
    wgdump = dram("wgdump", (128, 16384), BF16, kind="Internal")
    wddump = dram("wddump", (128, 16384), BF16, kind="Internal")
    itdump = dram("itdump", (128, 16384), BF16, kind="Internal")

    A = nc.sbuf_tensor("A", [128, 73728], BF16).__enter__()
    U = nc.sbuf_tensor("U", [128, 12288], U16).__enter__()
    Fa = nc.sbuf_tensor("Fa", [128, 4096], F32).__enter__()
    Tb = nc.sbuf_tensor("Tb", [128, 2048], F32).__enter__()
    ST = nc.sbuf_tensor("ST", [128, 32], F32).__enter__()
    epscol = nc.sbuf_tensor("epscol", [128, 1], F32).__enter__()
    lncol = nc.sbuf_tensor("lncol", [1, 1], F32).__enter__()
    negm = nc.sbuf_tensor("negm", [128, 1], F32).__enter__()
    onesr = nc.sbuf_tensor("onesr", [1, 512], BF16).__enter__()
    bias_sb = nc.sbuf_tensor("bias_sb", [1, 2048], BF16).__enter__()
    hsel = nc.sbuf_tensor("hsel_sb", [128, 256], BF16).__enter__()
    m16 = nc.sbuf_tensor("m16", [16, 8], F32).__enter__()
    m1 = nc.sbuf_tensor("m1", [1, 32], F32).__enter__()

    ps_a = nc.psum_tensor("ps_a", [128, 2048], F32).__enter__()
    ps_b = nc.psum_tensor("ps_b", [128, 1536], F32).__enter__()

    def a_(off, n):
        return A[:, off:off + n]

    kb.opi("vector", "memset", epscol[:], 1e-5)
    kb.opi("vector", "memset", lncol[:], -0.5 * math.log(float(D)))
    kb.opi("vector", "memset", m16[:, 0:2], 0.0)
    kb.dma("sync", onesr[:], onesb)
    kb.dma("sync", hsel[:], hsel_d)
    kb.barrier()

    kb.phase("p0_ln1")
    def ln_pass(src, n_tiles, dst_tm):
        junk = a_(0, 2048)
        norm_done = {0: 0, 1: 0}
        store_done = {0: 0, 1: 0}
        for i in range(n_tiles):
            buf = i % 2
            xtile = Fa[:, buf * 2048:(buf + 1) * 2048]
            if i >= 2:
                kb.wait("sync", "scalar", norm_done[buf])
            xv = kb.dma("sync", xtile, src[i * 128:(i + 1) * 128, :])
            kb.wait("scalar", "syncd", xv)
            s1, s2, mean, var, t0, inv, nmi = (ST[:, j:j + 1] for j in range(7))
            kb.op("scalar", "activation", junk, xtile, AFT.Identity, accum_out=s1)
            v_sc = kb.opi("scalar", "activation", junk, xtile, AFT.Square, accum_out=s2)
            kb.wait("vector", "scalar", v_sc)
            kb.tiny("vector", "tensor_scalar", mean, s1, 1.0 / E, None, ALU.mult)
            kb.tiny("vector", "tensor_scalar", var, s2, 1.0 / E, None, ALU.mult)
            kb.tiny("vector", "tensor_tensor", t0, mean, mean, ALU.mult)
            v_ve = kb.tiny("vector", "tensor_sub", var, var, t0)
            kb.wait("scalar", "vector", v_ve)
            kb.tiny("scalar", "activation", t0, var, AFT.Ln, bias=epscol[:])
            v_inv = kb.tiny("scalar", "activation", inv, t0, AFT.Exp, scale=-0.5)
            kb.wait("vector", "scalar", v_inv)
            kb.tiny("vector", "tensor_scalar", nmi, mean, -1.0, None, ALU.mult)
            v_nmi = kb.tiny("vector", "tensor_tensor", nmi, nmi, inv, ALU.mult)
            kb.wait("scalar", "vector", v_nmi)
            xn_tm = a_(2048 + buf * 2048, 2048)
            if i >= 2:
                kb.wait("scalar", "scalard", store_done[buf])
            norm_done[buf] = kb.opi("scalar", "activation", xn_tm, xtile,
                                    AFT.Identity, bias=nmi, scale=inv)
            kb.wait("scalar", "scalar", norm_done[buf])
            store_done[buf] = kb.dma(
                "scalar", dst_tm[i * 128:(i + 1) * 128, :], xn_tm)

    def fm_transpose(src_tm, dst_fm, ncols):
        st_done = {0: 0, 1: 0}
        for ec in range(EC):
            buf = ec % 2
            tile = a_(buf * 8192, ncols)
            if ec >= 2:
                kb.wait("sync", "syncd", st_done[buf])
            v_tr = kb.dma("sync", tile, src_tm[:, ec * 128:(ec + 1) * 128],
                          transpose=True)
            kb.wait("sync", "syncd", v_tr)
            st_done[buf] = kb.dma("sync", dst_fm[ec * 128:(ec + 1) * 128, :], tile)

    if PHASES >= 0:
        ln_pass(x, S // 128, xntm)
        kb.barrier()
        ln_pass(xq, TQ // 128, xnqtm)
        kb.barrier()
        fm_transpose(xntm, xnfm, S)
        fm_transpose(xnqtm, xnqfm, TQ)
        kb.barrier()

    def proj_kq(wname, bname, kind):
        WT_OFF = 16384
        wv = 0
        for ec in range(EC):
            wv = kb.dma("gpsimd", a_(WT_OFF + ec * 2048, 2048),
                        wT[wname][ec * 128:(ec + 1) * 128, :])
        bv_ = kb.dma("gpsimd", bias_sb[:], bvecs[bname])
        kb.wait("tensor", "gpsimdd", wv)
        kb.wait("tensor", "gpsimdd", bv_)
        src_fm = xnqfm if kind == "q" else xnfm
        n_tc = (TQ if kind == "q" else S) // 512
        cos_d = cos_q if kind == "q" else cos_kv
        sin_d = sinp_q if kind == "q" else sinp_kv
        dstc = qcache if kind == "q" else kcache
        xn_used = {0: 0, 1: 0}
        tab_used = {0: (0, 0), 1: (0, 0)}
        pk_free = {0: 0, 1: 0}
        kt_free = {0: 0, 1: 0}
        nrm_free = 0
        for tc in range(n_tc):
            buf = tc % 2
            if tc >= 2:
                kb.wait("sync", "tensor", xn_used[buf])
                kb.wait("sync", "vector", tab_used[buf][0])
                kb.wait("sync", "gpsimd", tab_used[buf][1])
            xv = 0
            for ec in range(EC):
                xv = kb.dma("sync", a_(buf * 8192 + ec * 512, 512),
                            src_fm[ec * 128:(ec + 1) * 128, tc * 512:(tc + 1) * 512])
            kb.dma("sync", Tb[:, buf * 1024:buf * 1024 + 512],
                   cos_d[:, tc * 512:(tc + 1) * 512])
            tv = kb.dma("sync", Tb[:, buf * 1024 + 512:buf * 1024 + 1024],
                        sin_d[:, tc * 512:(tc + 1) * 512])
            kb.wait("tensor", "syncd", xv)
            kb.wait("vector", "syncd", tv)
            kb.wait("tensor", "vector", nrm_free)
            for ds in range(EC):
                pbuf = (tc * EC + ds) % 2
                pk = ps_b[:, pbuf * 512:(pbuf + 1) * 512]
                kb.wait("tensor", "vector", pk_free[pbuf])
                for ec in range(EC):
                    kb.op("tensor", "matmul", pk,
                          a_(WT_OFF + ec * 2048 + ds * 128, 128),
                          a_(buf * 8192 + ec * 512, 512),
                          start=(ec == 0), stop=False)
                mv = kb.opi("tensor", "matmul", pk,
                            bias_sb[:, ds * 128:ds * 128 + 128], onesr[:],
                            start=False, stop=True)
                kb.wait("vector", "tensor", mv)
                qsht = Fa[:, pbuf * 512:(pbuf + 1) * 512]
                kb.op("vector", "stream_shuffle", qsht[0:64, :], pk[64:128, :], IDM)
                v_sh = kb.opi("vector", "stream_shuffle", qsht[64:128, :],
                              pk[0:64, :], IDM)
                t1 = a_(49152 + pbuf * 512, 512)
                v_t1 = kb.opi("vector", "tensor_tensor", t1, pk,
                              Tb[:, buf * 1024:buf * 1024 + 512], ALU.mult)
                pk_free[pbuf] = v_t1
                t2 = a_(50176 + pbuf * 512, 512)
                kb.wait("gpsimd", "vector", v_sh)
                kb.op("gpsimd", "tensor_tensor", t2, qsht,
                      Tb[:, buf * 1024 + 512:buf * 1024 + 1024], ALU.mult)
                ktile = a_(51200 + pbuf * 512, 512)
                kb.wait("gpsimd", "vector", v_t1)
                if kt_free[pbuf]:
                    kb.wait("gpsimd", "gpsimdd", kt_free[pbuf])
                v_kt = kb.opi("gpsimd", "tensor_tensor", ktile, t1, t2, ALU.add)
                kb.wait("gpsimd", "gpsimd", v_kt)
                kt_free[pbuf] = kb.dma(
                    "gpsimd", dstc[ds * 128:(ds + 1) * 128, tc * 512:(tc + 1) * 512],
                    ktile)
                sq = a_(52224 + pbuf * 512, 512)
                kb.wait("scalar", "gpsimd", v_kt)
                v_sq = kb.opi("scalar", "activation", sq, ktile, AFT.Square)
                kb.wait("tensor", "scalar", v_sq)
                v_mm = kb.opi("tensor", "matmul", ps_b[0:16, 1024:1536],
                              hsel[:, ds * 16:ds * 16 + 16], sq,
                              start=(ds == 0), stop=(ds == EC - 1),
                              skip_group_check=True)
            xn_used[buf] = v_mm
            tab_used[buf] = (kb.cnt["vector"], kb.cnt["gpsimd"])
            col = 0 if kind == "k" else 1
            kb.wait("vector", "tensor", v_mm)
            kb.tiny("vector", "reduce_max", m16[:, 2:3], ps_b[0:16, 1024:1536], AX.X)
            nrm_free = kb.tiny("vector", "tensor_max", m16[:, col:col + 1],
                               m16[:, col:col + 1], m16[:, 2:3])

    def proj_v():
        WT_OFF = 16384
        wv = 0
        for ec in range(EC):
            wv = kb.dma("gpsimd", a_(WT_OFF + ec * 2048, 2048),
                        wT["wvT"][ec * 128:(ec + 1) * 128, :])
        bv_ = kb.dma("gpsimd", bias_sb[:], bvecs["bv"])
        kb.wait("tensor", "gpsimdd", wv)
        kb.wait("tensor", "gpsimdd", bv_)
        xn_used = {0: 0, 1: 0}
        pv_free = {0: 0, 1: 0}
        vt_free = {0: 0, 1: 0}
        for tc in range(S // 512):
            buf = tc % 2
            if tc >= 2:
                kb.wait("sync", "tensor", xn_used[buf])
            xv = 0
            for ec in range(EC):
                xv = kb.dma("sync", a_(buf * 8192 + ec * 512, 512),
                            xnfm[ec * 128:(ec + 1) * 128, tc * 512:(tc + 1) * 512])
            kb.wait("tensor", "syncd", xv)
            for st_i in range(4):
                vtb = (tc * 4 + st_i) % 2
                vtile = a_(49152 + vtb * 2064, 2064)
                vt_r = vtile.rearrange("p (h c) -> p h c", h=16)
                if vt_free[vtb]:
                    kb.wait("vector", "scalard", vt_free[vtb])
                v_ms = kb.opi("vector", "memset", vt_r[:, :, 128:129], 1.0)
                for dc in range(4):
                    pbuf = (tc * 16 + st_i * 4 + dc) % 2
                    pv = ps_b[:, pbuf * 512:(pbuf + 1) * 512]
                    kb.wait("tensor", "scalar", pv_free[pbuf])
                    for ec in range(EC):
                        kb.op("tensor", "matmul", pv,
                              a_(buf * 8192 + ec * 512 + st_i * 128, 128),
                              a_(WT_OFF + ec * 2048 + dc * 512, 512),
                              start=(ec == 0), stop=False)
                    mv = kb.opi("tensor", "matmul", pv, onesr[:, 0:128],
                                bias_sb[:, dc * 512:(dc + 1) * 512],
                                start=False, stop=True)
                    kb.wait("scalar", "tensor", mv)
                    if dc == 0:
                        kb.wait("scalar", "vector", v_ms)
                    pv_free[pbuf] = kb.opi(
                        "scalar", "activation",
                        vt_r[:, dc * 4:dc * 4 + 4, 0:128],
                        pv.rearrange("p (a c) -> p a c", a=4), AFT.Identity)
                kb.wait("scalar", "scalar", kb.cnt["scalar"])
                vt_free[vtb] = kb.dma(
                    "scalar",
                    vcache[tc * 512 + st_i * 128: tc * 512 + (st_i + 1) * 128, :],
                    vtile)
            xn_used[buf] = kb.cnt["tensor"]

    if PHASES >= 1:
        kb.phase("p1k")
        proj_kq("wkT", "bk", "k")
        kb.barrier()
        kb.phase("p1v")
        proj_v()
        kb.barrier()
        kb.phase("p1q")
        proj_kq("wqT", "bq", "q")
        kb.barrier()
        kb.tiny("vector", "tensor_tensor", m16[:, 3:4], m16[:, 0:1], m16[:, 1:2],
                ALU.mult)
        kb.wait("gpsimd", "vector", kb.cnt["vector"])
        mv1 = kb.dma("gpsimd", mtmp, m16[:, 3:4])
        kb.wait("sync", "gpsimdd", mv1)
        mv2 = kb.dma("sync", m1[:, 0:16], mtmp.rearrange("a b -> b a"))
        kb.wait("vector", "syncd", mv2)
        v_mx = kb.tiny("vector", "reduce_max", m1[:, 16:17], m1[:, 0:16], AX.X)
        kb.wait("scalar", "vector", v_mx)
        kb.tiny("scalar", "activation", m1[:, 17:18], m1[:, 16:17], AFT.Ln)
        v_m = kb.tiny("scalar", "activation", m1[:, 18:19], m1[:, 17:18], AFT.Exp,
                      scale=0.5, bias=lncol[:])
        kb.wait("vector", "scalar", v_m)
        v_neg = kb.opi("vector", "tensor_scalar", m1[:, 19:20], m1[:, 18:19],
                       -1.0, None, ALU.mult)
        kb.wait("gpsimd", "vector", v_neg)
        mv3 = kb.dma("gpsimd", mglob, m1[:, 19:20])
        kb.wait("sync", "gpsimdd", mv3)
        kb.dma("sync", negm[:], mglob.broadcast_to([128, 1]))
        kb.barrier()

    SM = nc.sbuf_tensor("SM", [128, 48], F32).__enter__()
    if PHASES >= 2:
        kb.phase("p2_attn")
        for half in (0, 1):
            kv = kb.dma("sync", A[:, 0:32768].rearrange("d (h s) -> d h s", h=16),
                        kcache.rearrange("(h d) s -> d h s", d=128)
                        [:, :, half * 2048:(half + 1) * 2048])
            kv2 = kb.dma("sync", A[:, 32768:65792].rearrange("p (a c) -> p a c", a=16),
                         vcache.rearrange("(a p) c -> p a c", p=128)
                         [:, half * 16:(half + 1) * 16, :])
            kb.wait("tensor", "syncd", kv)
            kb.wait("tensor", "syncd", kv2)
            qsb_used = {0: 0, 1: 0}
            sc_free = {0: 0, 1: 0}
            p_free = {0: 0, 1: 0}
            av_free_s = 0
            av_free_v = 0
            sp_used = 0
            avn_used = 0
            spv = 0
            for qb in range(16):
                qbuf = qb % 2
                if qb >= 2:
                    kb.wait("sync", "tensor", qsb_used[qbuf])
                qv = kb.dma("sync",
                            A[:, 65792 + qbuf * 2048:65792 + (qbuf + 1) * 2048]
                            .rearrange("d (h t) -> d h t", h=16),
                            qcache.rearrange("(h d) t -> d h t", d=128)
                            [:, :, qb * 128:(qb + 1) * 128])
                kb.wait("tensor", "syncd", qv)
                if half == 1:
                    kb.wait("sync", "vector", sp_used)
                    spv = kb.dma("sync", Fa[:, 0:2064],
                                 avsp[qb * 128:(qb + 1) * 128, :])
                kb.wait("tensor", "scalar", av_free_s)
                kb.wait("tensor", "vector", av_free_v)
                v_last = 0
                iters = [(h, grp) for h in range(16) for grp in range(4)]

                def emit_scores(k):
                    h, grp = iters[k]
                    pb = k % 2
                    sc = ps_b[:, 512 * (1 + pb):512 * (2 + pb)]
                    kb.wait("tensor", "scalar", sc_free[pb])
                    for ss in range(4):
                        emit = kb.opi if ss == 3 else kb.op
                        r = emit("tensor", "matmul",
                                 sc[:, ss * 128:(ss + 1) * 128],
                                 a_(h * 2048 + grp * 512 + ss * 128, 128),
                                 a_(65792 + qbuf * 2048 + h * 128, 128),
                                 start=True, stop=True)
                    return r

                def emit_exp(k, v_scmm):
                    pb = k % 2
                    sc = ps_b[:, 512 * (1 + pb):512 * (2 + pb)]
                    pgrp = a_(69888 + pb * 512, 512)
                    kb.wait("scalar", "tensor", max(v_scmm, p_free[pb]))
                    sc_free[pb] = kb.opi("scalar", "activation", pgrp, sc,
                                         AFT.Exp,
                                         scale=float(1.0 / math.sqrt(D)),
                                         bias=negm[:])
                    return sc_free[pb]

                def emit_av(k):
                    h, grp = iters[k]
                    pb = k % 2
                    pgrp = a_(69888 + pb * 512, 512)
                    kb.wait("tensor", "scalar", sc_free[pb])
                    for ss in range(4):
                        st_idx = grp * 4 + ss
                        vb = 32768 + st_idx * 2064 + 129 * h
                        kb.op("tensor", "matmul",
                              ps_a[:, h * 128:(h + 1) * 128],
                              pgrp[:, ss * 128:(ss + 1) * 128],
                              A[:, vb:vb + 128],
                              start=(ss == 0 and grp == 0 and (h % 4) == 0),
                              stop=False, skip_group_check=True)
                        p_free[pb] = kb.opi(
                            "tensor", "matmul", ps_b[:, h:h + 1],
                            pgrp[:, ss * 128:(ss + 1) * 128],
                            A[:, vb + 128:vb + 129],
                            start=(ss == 0 and grp == 0 and h == 0),
                            stop=False, skip_group_check=True)
                    return p_free[pb]

                v_sc_prev = emit_scores(0)
                emit_exp(0, v_sc_prev)
                for k in range(1, 64):
                    v_sc_k = emit_scores(k)
                    emit_exp(k, v_sc_k)
                    v_last = emit_av(k - 1)
                v_last = emit_av(63)
                if half == 0:
                    kb.wait("scalar", "tensor", v_last)
                    kb.op("scalar", "activation", Fa[:, 0:2048], ps_a[:], AFT.Identity)
                    av_free_s = kb.opi("scalar", "activation", Fa[:, 2048:2064],
                                       ps_b[:, 0:16], AFT.Identity)
                    kb.wait("scalar", "scalar", av_free_s)
                    kb.dma("scalar", avsp[qb * 128:(qb + 1) * 128, :], Fa[:, 0:2064])
                else:
                    kb.wait("vector", "tensor", v_last)
                    kb.wait("vector", "syncd", spv)
                    kb.tiny("vector", "tensor_add", SM[:, 0:16], ps_b[:, 0:16],
                            Fa[:, 2048:2064])
                    kb.tiny("vector", "reciprocal", SM[:, 16:32], SM[:, 0:16])
                    av_free_v = kb.opi("vector", "tensor_add", Fa[:, 0:2048],
                                       ps_a[:], Fa[:, 0:2048])
                    avn_sb = a_(70912, 2048)
                    if avn_used:
                        kb.wait("vector", "gpsimdd", avn_used)
                    v_mul = 0
                    for h in range(16):
                        emit = kb.opi if h == 15 else kb.op
                        r = emit("vector", "tensor_tensor",
                                 avn_sb[:, h * 128:(h + 1) * 128],
                                 Fa[:, h * 128:(h + 1) * 128],
                                 SM[:, 16 + h:17 + h].broadcast_to([128, 128]),
                                 ALU.mult)
                        if h == 15:
                            v_mul = r
                    sp_used = v_mul
                    kb.wait("gpsimd", "vector", v_mul)
                    avn_used = kb.dma("gpsimd", avn[qb * 128:(qb + 1) * 128, :],
                                      avn_sb)
                qsb_used[qbuf] = kb.cnt["tensor"]
            kb.barrier()

    if PHASES >= 3:
        kb.phase("p25_oproj")
        wv = 0
        for ec in range(EC):
            wv = kb.dma("gpsimd", a_(ec * 2048, 2048),
                        wT["woT"][ec * 128:(ec + 1) * 128, :])
        bv_ = kb.dma("gpsimd", bias_sb[:], bvecs["bo"])
        kb.wait("tensor", "gpsimdd", wv)
        kb.wait("tensor", "gpsimdd", bv_)
        junk2 = a_(34816, 512)
        xn2_used_c = 0
        v_x1_prev = 0
        avt_used = 0
        x1_used = 0
        xn2_used = 0
        po_free = {0: 0, 1: 0}
        for qb in range(16):
            avv = 0
            for ec in range(EC):
                if ec < 2:
                    kb.wait("sync", "tensor", avt_used)
                kb.dma("sync", a_(32768 + ec * 128, 128),
                       avn[qb * 128:(qb + 1) * 128, ec * 128:(ec + 1) * 128],
                       transpose=True)
                avv = kb.cnt["syncd"]
            kb.wait("sync", "scalar", xn2_used_c)
            xqv = kb.dma("sync", Fa[:, 2048:4096], xq[qb * 128:(qb + 1) * 128, :])
            kb.wait("tensor", "syncd", avv)
            kb.wait("scalar", "vector", v_x1_prev)
            for eoc in range(4):
                pb = eoc % 2
                po = ps_b[:, pb * 512:(pb + 1) * 512]
                kb.wait("tensor", "scalar", po_free[pb])
                for ec in range(EC):
                    kb.op("tensor", "matmul", po, a_(32768 + ec * 128, 128),
                          a_(ec * 2048 + eoc * 512, 512),
                          start=(ec == 0), stop=False)
                mv = kb.opi("tensor", "matmul", po, onesr[:, 0:128],
                            bias_sb[:, eoc * 512:(eoc + 1) * 512],
                            start=False, stop=True)
                kb.wait("scalar", "tensor", mv)
                kb.op("scalar", "activation", Fa[:, eoc * 512:(eoc + 1) * 512],
                      po, AFT.Identity, accum_out=ST[:, 8 + eoc:9 + eoc])
                po_free[pb] = kb.opi("scalar", "activation", junk2, po,
                                     AFT.Square, accum_out=ST[:, 12 + eoc:13 + eoc])
            avt_used = mv
            s1, s2, mean, var, t0, inv, nmi = (ST[:, j:j + 1] for j in range(7))
            kb.wait("vector", "scalar", po_free[pb])
            kb.tiny("vector", "reduce_sum", s1, ST[:, 8:12], AX.X)
            kb.tiny("vector", "reduce_sum", s2, ST[:, 12:16], AX.X)
            kb.tiny("vector", "tensor_scalar", mean, s1, 1.0 / E, None, ALU.mult)
            kb.tiny("vector", "tensor_scalar", var, s2, 1.0 / E, None, ALU.mult)
            kb.tiny("vector", "tensor_tensor", t0, mean, mean, ALU.mult)
            v_ve = kb.tiny("vector", "tensor_sub", var, var, t0)
            kb.wait("scalar", "vector", v_ve)
            kb.tiny("scalar", "activation", t0, var, AFT.Ln, bias=epscol[:])
            v_inv = kb.tiny("scalar", "activation", inv, t0, AFT.Exp, scale=-0.5)
            kb.wait("vector", "scalar", v_inv)
            kb.tiny("vector", "tensor_scalar", nmi, mean, -1.0, None, ALU.mult)
            v_nmi = kb.tiny("vector", "tensor_tensor", nmi, nmi, inv, ALU.mult)
            kb.wait("scalar", "vector", v_nmi)
            v_n2 = kb.opi("scalar", "activation", Fa[:, 0:2048], Fa[:, 0:2048],
                          AFT.Identity, bias=nmi, scale=inv)
            kb.wait("vector", "scalar", v_n2)
            kb.wait("vector", "syncd", xqv)
            if x1_used:
                kb.wait("vector", "gpsimdd", x1_used)
            v_x1 = kb.opi("vector", "tensor_add", Fa[:, 2048:4096],
                          Fa[:, 2048:4096], Fa[:, 0:2048])
            kb.wait("gpsimd", "vector", v_x1)
            x1_used = kb.dma("gpsimd", x1sp[qb * 128:(qb + 1) * 128, :],
                             Fa[:, 2048:4096])
            kb.wait("scalar", "vector", v_x1)
            for c4 in range(4):
                kb.op("scalar", "activation", junk2,
                      Fa[:, 2048 + c4 * 512:2048 + (c4 + 1) * 512],
                      AFT.Identity, accum_out=ST[:, 8 + c4:9 + c4])
            for c4 in range(4):
                r = (kb.opi if c4 == 3 else kb.op)(
                    "scalar", "activation", junk2,
                    Fa[:, 2048 + c4 * 512:2048 + (c4 + 1) * 512],
                    AFT.Square, accum_out=ST[:, 12 + c4:13 + c4])
            v_st3 = r
            kb.wait("vector", "scalar", v_st3)
            kb.tiny("vector", "reduce_sum", s1, ST[:, 8:12], AX.X)
            kb.tiny("vector", "reduce_sum", s2, ST[:, 12:16], AX.X)
            kb.tiny("vector", "tensor_scalar", mean, s1, 1.0 / E, None, ALU.mult)
            kb.tiny("vector", "tensor_scalar", var, s2, 1.0 / E, None, ALU.mult)
            kb.tiny("vector", "tensor_tensor", t0, mean, mean, ALU.mult)
            v_ve = kb.tiny("vector", "tensor_sub", var, var, t0)
            kb.wait("scalar", "vector", v_ve)
            kb.tiny("scalar", "activation", t0, var, AFT.Ln, bias=epscol[:])
            v_inv = kb.tiny("scalar", "activation", inv, t0, AFT.Exp, scale=-0.5)
            kb.wait("vector", "scalar", v_inv)
            kb.tiny("vector", "tensor_scalar", nmi, mean, -1.0, None, ALU.mult)
            v_nmi = kb.tiny("vector", "tensor_tensor", nmi, nmi, inv, ALU.mult)
            kb.wait("scalar", "vector", v_nmi)
            xn2sb = a_(35328, 2048)
            if xn2_used:
                kb.wait("scalar", "scalard", xn2_used)
            v_xn2 = kb.opi("scalar", "activation", xn2sb, Fa[:, 2048:4096],
                           AFT.Identity, bias=nmi, scale=inv)
            xn2_used_c = v_xn2
            v_x1_prev = v_x1
            kb.wait("scalar", "scalar", v_xn2)
            xn2_used = kb.dma("scalar", xn2tm[qb * 128:(qb + 1) * 128, :], xn2sb)
        kb.barrier()

    def nib_ops(r):
        masks = [(0x0007, 6, ALU.logical_shift_left, 0x0008, 12),
                 (0x0070, 2, ALU.logical_shift_left, 0x0080, 8),
                 (0x0700, 2, ALU.logical_shift_right, 0x0800, 4),
                 (0x7000, 6, ALU.logical_shift_right, 0x8000, 0)]
        return masks[r]

    if PHASES >= 4:
        kb.phase("p3_scaleprep")
        P126 = float(2.0 ** 126)
        for srcT, dstE, nrow in ((sgT, sge, 64), (suT, sue, 64), (sdT, sde, 128)):
            nchunk = 4
            for ci in range(nchunk):
                if nrow == 64:
                    src_ap = srcT[:, ci * 4096:(ci + 1) * 4096]
                    ftile = Fa[0:64, :]
                    btile = a_(0, 4096)[0:64, :]
                else:
                    src_ap = srcT[ci * 128:(ci + 1) * 128, :]
                    ftile = Fa[:, 0:2048]
                    btile = a_(0, 2048)
                kb.wait("sync", "vector", kb.cnt["vector"])
                lv = kb.dma("sync", ftile, src_ap)
                kb.wait("vector", "syncd", lv)
                v_m2 = kb.opi("vector", "tensor_scalar", btile, ftile, P126,
                              None, ALU.mult)
                kb.wait("gpsimd", "vector", v_m2)
                stv = 0
                for e8 in range(8):
                    if nrow == 64:
                        dst_ap = dstE.rearrange("(a e) c -> a e c", e=8)[
                            :, e8, ci * 4096:(ci + 1) * 4096]
                    else:
                        dst_ap = dstE.rearrange("(a e) c -> a e c", e=8)[
                            ci * 128:(ci + 1) * 128, e8, :]
                    stv = kb.dma("gpsimd", dst_ap, btile)
                kb.wait("vector", "gpsimdd", stv)
        kb.barrier()
        fm_transpose(xn2tm, xn2fm, TQ)
        kb.barrier()
        for bb in range(EC):
            rstart = 512 * (bb % 4) + bb // 4
            v_t = kb.dma("sync", a_(bb * 2048, 2048),
                         xn2fm[rstart:rstart + 509:4, :])
        kb.wait("tensor", "syncd", v_t)
        kb.barrier()

        kb.phase("p3_ffn")
        U32 = mybir.dt.uint32
        R1 = 32768
        WDT = 49152
        ITR = 57344
        SCG, SCU, SCD = 65536, 67584, 69632
        EM0, EM1, SGT_, GGT = 71680, 72192, 72704, 73216
        em2 = [a_(EM0, 512), a_(EM1, 512)]
        sg32 = a_(SGT_, 512).bitcast(U32)
        gg = a_(GGT, 512)
        tb16 = Tb[:].bitcast(BF16)

        def nib32(r):
            msk, sh, shop, msk2, sh2 = nib_ops(r)
            return (msk | (msk << 16), sh, shop, msk2 | (msk2 << 16), sh2)

        acc_dma = [0] * 16
        em_free = [(0, 0), (0, 0)]
        u_free = {0: 0, 1: 0}
        ud_free = {0: 0, 1: 0}
        nidx = 0
        v_gu_last = 0
        gg_free = 0
        v_dn = 0
        dbuf_free = {0: 0, 1: 0}
        psd_free = {0: 0, 1: 0}
        psg_free = {0: 0, 1: 0}
        psu_free = {0: 0, 1: 0}
        scl_gu_free = (0, 0)
        scl_d_free = (0, 0)
        vv_w = {"vector": 0, "gpsimd": 0}

        def dequant(n_sub, usrc_u16, wslot, scslot, guard_t):
            nonlocal nidx
            usrc = usrc_u16.bitcast(U32)
            for r in range(4):
                msk, sh, shop, msk2, sh2 = nib32(r)
                eb = em2[nidx % 2]
                eb32 = eb.bitcast(U32)
                kb.wait("vector", "vector", em_free[nidx % 2][0])
                kb.wait("vector", "gpsimd", em_free[nidx % 2][1])
                if guard_t:
                    kb.wait("vector", "tensor", guard_t)
                kb.op("vector", "tensor_scalar", eb32, usrc,
                      msk, sh, ALU.bitwise_and, shop)
                if sh2:
                    kb.op("vector", "tensor_scalar", sg32, usrc,
                          msk2, sh2, ALU.bitwise_and, ALU.logical_shift_left)
                else:
                    kb.op("vector", "tensor_scalar", sg32, usrc,
                          msk2, None, ALU.bitwise_and)
                v_or = kb.opi("vector", "tensor_tensor", eb32, eb32,
                              sg32, ALU.bitwise_or)
                eng = "gpsimd" if nidx % 2 == 0 else "vector"
                if eng == "gpsimd":
                    kb.wait("gpsimd", "vector", v_or)
                    if guard_t:
                        kb.wait("gpsimd", "tensor", guard_t)
                vv_w[eng] = kb.opi(eng, "tensor_tensor",
                                   a_(wslot(r), 512),
                                   eb.bitcast(BF16), scslot, ALU.mult)
                em_free[nidx % 2] = ((vv_w["vector"], kb.cnt["gpsimd"])
                                     if eng == "vector" else
                                     (kb.cnt["vector"], vv_w["gpsimd"]))
                nidx += 1

        for j in range(32):
            ub = j % 2
            scb = ub
            scg_ap = (lambda a4: a_(SCG + a4 * 512, 512)) if scb == 0 else \
                     (lambda a4: tb16[:, a4 * 512:(a4 + 1) * 512])
            scu_ap = (lambda a4: a_(SCU + a4 * 512, 512)) if scb == 0 else \
                     (lambda a4: tb16[:, 2048 + a4 * 512:2048 + (a4 + 1) * 512])
            kb.wait("sync", "vector", u_free[ub])
            pv = 0
            for a4 in range(4):
                pv = kb.dma("sync", U[:, ub * 2048 + a4 * 512:ub * 2048 + (a4 + 1) * 512],
                            gpT[a4 * 128:(a4 + 1) * 128, j * 512:(j + 1) * 512])
                pv = kb.dma("sync", U[:, 4096 + ub * 2048 + a4 * 512:4096 + ub * 2048 + (a4 + 1) * 512],
                            upT[a4 * 128:(a4 + 1) * 128, j * 512:(j + 1) * 512])
            kb.wait("sync", "vector", scl_gu_free[0])
            kb.wait("sync", "gpsimd", scl_gu_free[1])
            sv = 0
            for a4 in range(4):
                sv = kb.dma("sync", scg_ap(a4),
                            sge[a4 * 128:(a4 + 1) * 128, j * 512:(j + 1) * 512])
                sv = kb.dma("sync", scu_ap(a4),
                            sue[a4 * 128:(a4 + 1) * 128, j * 512:(j + 1) * 512])
            kb.wait("sync", "vector", ud_free[ub])
            kb.wait("sync", "vector", scl_d_free[0])
            kb.wait("sync", "gpsimd", scl_d_free[1])
            dv = kb.dma("sync", U[:, 8192 + ub * 2048:8192 + (ub + 1) * 2048],
                        dpT[j * 128:(j + 1) * 128, :])
            sdv = kb.dma("sync", a_(SCD, 2048), sde[j * 128:(j + 1) * 128, :])
            kb.wait("vector", "syncd", max(pv, sv))
            kb.wait("gpsimd", "syncd", max(pv, sv))
            for mi, (uoff, woff, scf) in enumerate(
                    ((0, R1, scg_ap), (4096, R1 + 8192, scu_ap))):
                for a4 in range(4):
                    usrc = U[:, uoff + ub * 2048 + a4 * 512:uoff + ub * 2048 + (a4 + 1) * 512]
                    dequant(nidx, usrc,
                            (lambda r, a4=a4, woff=woff: woff + (4 * r + a4) * 512),
                            scf(a4), v_gu_last)
            scl_gu_free = (vv_w["vector"], vv_w["gpsimd"])
            u_free[ub] = kb.cnt["vector"]
            v_w_v, v_w_g = vv_w["vector"], vv_w["gpsimd"]
            kb.wait("vector", "syncd", max(dv, sdv))
            kb.wait("gpsimd", "syncd", max(dv, sdv))
            for c4 in range(4):
                usrc = U[:, 8192 + ub * 2048 + c4 * 512:8192 + ub * 2048 + (c4 + 1) * 512]
                dequant(nidx, usrc,
                        (lambda r, c4=c4: WDT + r * 2048 + c4 * 512),
                        a_(SCD + c4 * 512, 512), v_dn)
            scl_d_free = (vv_w["vector"], vv_w["gpsimd"])
            ud_free[ub] = kb.cnt["vector"]
            v_wd_v, v_wd_g = vv_w["vector"], vv_w["gpsimd"]
            if j == 0 and "wddump" in dbg:
                kb.wait("gpsimd", "gpsimd", v_wd_g)
                kb.wait("gpsimd", "vector", v_wd_v)
                kb.dma("gpsimd", wddump[:, 0:8192], a_(WDT, 8192))
            kb.wait("tensor", "gpsimd", v_w_g)
            kb.wait("tensor", "vector", v_w_v)
            first_inter = True
            for rn in range(4):
                for tch in range(4):
                    gpb = tch % 2
                    psg = ps_a[:, gpb * 512:(gpb + 1) * 512]
                    psu = ps_a[:, 1024 + gpb * 512:1024 + (gpb + 1) * 512]
                    kb.wait("tensor", "scalar", psg_free[gpb])
                    kb.wait("tensor", "vector", psu_free[gpb])
                    for bb in range(EC):
                        kb.op("tensor", "matmul", psg,
                              A[:, R1 + bb * 512 + rn:R1 + (bb + 1) * 512:4],
                              a_(bb * 2048 + tch * 512, 512),
                              start=(bb == 0), stop=(bb == EC - 1))
                    v_gu = 0
                    for bb in range(EC):
                        emit = kb.opi if bb == EC - 1 else kb.op
                        r2 = emit("tensor", "matmul", psu,
                                  A[:, R1 + 8192 + bb * 512 + rn:R1 + 8192 + (bb + 1) * 512:4],
                                  a_(bb * 2048 + tch * 512, 512),
                                  start=(bb == 0), stop=(bb == EC - 1))
                        if bb == EC - 1:
                            v_gu = r2
                    v_gu_last = v_gu
                    kb.wait("scalar", "tensor", v_gu)
                    kb.wait("scalar", "vector", gg_free)
                    psg_free[gpb] = kb.opi("scalar", "activation", gg, psg,
                                           AFT.Gelu_apprx_tanh)
                    kb.wait("vector", "scalar", psg_free[gpb])
                    if first_inter:
                        kb.wait("vector", "tensor", v_dn)
                        first_inter = False
                    psu_free[gpb] = kb.opi(
                        "vector", "tensor_tensor",
                        a_(ITR + rn * 2048 + tch * 512, 512),
                        psu, gg, ALU.mult)
                    gg_free = psu_free[gpb]
            v_intl = gg_free
            if j == 0 and "wddump" in dbg:
                kb.wait("gpsimd", "vector", v_intl)
                kb.dma("gpsimd", itdump[:, 0:8192], a_(ITR, 8192))
            kb.wait("tensor", "gpsimd", v_wd_g)
            kb.wait("tensor", "vector", max(v_wd_v, v_intl))
            for tsub in range(16):
                dbuf = tsub % 2
                kb.wait("scalar", "gpsimdd", dbuf_free[dbuf])
                for hc in range(4):
                    pdb = hc % 2
                    psd = ps_b[:, pdb * 512:(pdb + 1) * 512]
                    kb.wait("tensor", "scalar", psd_free[pdb])
                    for r in range(4):
                        emit = kb.opi if r == 3 else kb.op
                        r3 = emit("tensor", "matmul", psd,
                                  a_(ITR + r * 2048 + tsub * 128, 128),
                                  a_(WDT + r * 2048 + hc * 512, 512),
                                  start=(r == 0), stop=(r == 3))
                        if r == 3:
                            v_dn = r3
                    kb.wait("scalar", "tensor", v_dn)
                    psd_free[pdb] = kb.opi(
                        "scalar", "activation",
                        Fa[:, dbuf * 2048 + hc * 512:dbuf * 2048 + (hc + 1) * 512],
                        psd, AFT.Identity)
                kb.wait("gpsimd", "scalar", psd_free[pdb])
                if acc_dma[tsub]:
                    kb.wait("gpsimd", "gpsimdd", acc_dma[tsub])
                kw = {} if j == 0 else dict(accum_op=ALU.add)
                acc_dma[tsub] = kb.dma(
                    "gpsimd", acc[tsub * 128:(tsub + 1) * 128, :],
                    Fa[:, dbuf * 2048:(dbuf + 1) * 2048],
                    max_dma_last_dim=8192, **kw)
                dbuf_free[dbuf] = acc_dma[tsub]
        kb.barrier()

    if PHASES >= 5:
        kb.phase("p4_final")
        junk4 = a_(16384, 2048)
        out_done = {0: 0, 1: 0}
        norm4 = {0: 0, 1: 0}
        addv4 = {0: 0, 1: 0}
        for tsub in range(16):
            buf = tsub % 2
            acct = Fa[:, buf * 2048:(buf + 1) * 2048]
            x1t = a_(buf * 4096, 4096).bitcast(F32)
            outt = a_(8192 + buf * 4096, 4096).bitcast(F32)
            if tsub >= 2:
                kb.wait("sync", "scalar", norm4[buf])
                kb.wait("sync", "vector", addv4[buf])
            av1 = kb.dma("sync", acct, acc[tsub * 128:(tsub + 1) * 128, :])
            xv1 = kb.dma("sync", x1t, x1sp[tsub * 128:(tsub + 1) * 128, :])
            kb.wait("scalar", "syncd", av1)
            s1, s2, mean, var, t0, inv, nmi = (ST[:, j:j + 1] for j in range(7))
            kb.op("scalar", "activation", junk4, acct, AFT.Identity, accum_out=s1)
            v_sc = kb.opi("scalar", "activation", junk4, acct, AFT.Square,
                          accum_out=s2)
            kb.wait("vector", "scalar", v_sc)
            kb.tiny("vector", "tensor_scalar", mean, s1, 1.0 / E, None, ALU.mult)
            kb.tiny("vector", "tensor_scalar", var, s2, 1.0 / E, None, ALU.mult)
            kb.tiny("vector", "tensor_tensor", t0, mean, mean, ALU.mult)
            v_ve = kb.tiny("vector", "tensor_sub", var, var, t0)
            kb.wait("scalar", "vector", v_ve)
            kb.tiny("scalar", "activation", t0, var, AFT.Ln, bias=epscol[:])
            v_inv = kb.tiny("scalar", "activation", inv, t0, AFT.Exp, scale=-0.5)
            kb.wait("vector", "scalar", v_inv)
            kb.tiny("vector", "tensor_scalar", nmi, mean, -1.0, None, ALU.mult)
            v_nmi = kb.tiny("vector", "tensor_tensor", nmi, nmi, inv, ALU.mult)
            kb.wait("scalar", "vector", v_nmi)
            norm4[buf] = kb.opi("scalar", "activation", acct, acct,
                                AFT.Identity, bias=nmi, scale=inv)
            kb.wait("vector", "scalar", norm4[buf])
            kb.wait("vector", "syncd", xv1)
            if tsub >= 2:
                kb.wait("vector", "gpsimdd", out_done[buf])
            v_o = kb.opi("vector", "tensor_add", outt, acct, x1t)
            addv4[buf] = v_o
            kb.wait("gpsimd", "vector", v_o)
            out_done[buf] = kb.dma("gpsimd", out[tsub * 128:(tsub + 1) * 128, :],
                                   outt)
        kb.barrier()

    if PHASES < 5:
        v_o = 0
        for i in range(TQ // 128):
            v_o = kb.dma("sync", out[i * 128:(i + 1) * 128, :],
                         Fa[:, 0:2048])
        kb.wait("sync", "syncd", v_o)

    return kb


def host_prepare(L0_attn_x, L0_attn_Wq, L0_attn_bq, L0_attn_Wk, L0_attn_bk,
                 L0_attn_Wv, L0_attn_bv, L0_attn_Wo, L0_attn_bo,
                 L0_ffn_gate_W_packed, L0_ffn_gate_scales, L0_ffn_up_W_packed,
                 L0_ffn_up_scales, L0_ffn_down_W_packed, L0_ffn_down_scales):
    x = np.ascontiguousarray(np.asarray(L0_attn_x, dtype=np.float32))

    def wt(w):
        return np.ascontiguousarray(np.asarray(w, np.float32).T)

    def pk(p):
        a = np.ascontiguousarray(np.asarray(p, np.int32)).view(np.uint8)[:, 0::4]
        a = np.ascontiguousarray(a).view(np.uint16)
        return np.ascontiguousarray(a.T)

    def sc(s):
        return np.ascontiguousarray(np.asarray(s, np.float32).T)

    shared = {
        "wqT": wt(L0_attn_Wq), "wkT": wt(L0_attn_Wk),
        "wvT": wt(L0_attn_Wv), "woT": wt(L0_attn_Wo),
        "bq": np.asarray(L0_attn_bq, np.float32).reshape(1, E),
        "bk": np.asarray(L0_attn_bk, np.float32).reshape(1, E),
        "bv": np.asarray(L0_attn_bv, np.float32).reshape(1, E),
        "bo": np.asarray(L0_attn_bo, np.float32).reshape(1, E),
        "gpT": pk(L0_ffn_gate_W_packed), "upT": pk(L0_ffn_up_W_packed),
        "dpT": pk(L0_ffn_down_W_packed),
        "sgT": sc(L0_ffn_gate_scales), "suT": sc(L0_ffn_up_scales),
        "sdT": sc(L0_ffn_down_scales),
        "onesb": np.ones((1, 512), np.float32).astype(BFNP),
    }
    inv_freq = 1.0 / 10000.0 ** (np.arange(0, D, 2, dtype=np.float32) / D)
    pos = np.arange(S, dtype=np.float32)
    freqs = np.outer(pos, inv_freq)
    emb = np.concatenate([freqs, freqs], axis=-1)
    cos = np.cos(emb).astype(np.float32)
    sin = np.sin(emb).astype(np.float32)
    sinp = sin.copy()
    sinp[:, :64] *= -1.0
    cos_fm = np.ascontiguousarray(cos.T)
    sinp_fm = np.ascontiguousarray(sinp.T)
    shared["cos_kv"] = cos_fm
    shared["sinp_kv"] = sinp_fm
    hs = np.zeros((128, 256), np.float32)
    for h in range(16):
        hs[:, 16 * h + h] = 1.0
    shared["hsel"] = hs.astype(BFNP)

    in_maps = []
    for c in range(8):
        b = c // 2
        q0 = TQ * (c % 2)
        mcore = dict(shared)
        mcore["x"] = np.ascontiguousarray(x[b])
        mcore["xq"] = np.ascontiguousarray(x[b, q0:q0 + TQ])
        mcore["cos_q"] = np.ascontiguousarray(cos_fm[:, q0:q0 + TQ])
        mcore["sinp_q"] = np.ascontiguousarray(sinp_fm[:, q0:q0 + TQ])
        in_maps.append(mcore)
    return in_maps


def make_nc(dbg=()):
    nc = bass.Bass("TRN2", target_bir_lowering=False)
    kb = build_core_program(nc, dbg=dbg)
    with nc.Block() as block:
        for eng in ENGS:
            def body(e, eng=eng):
                marks = list(kb.marks[eng]) + [(len(kb.ops[eng]), None)]
                mi = 0
                cur = None
                import contextlib
                ctx = contextlib.nullcontext()
                for i, fn in enumerate(kb.ops[eng]):
                    while mi < len(marks) - 1 and i >= marks[mi][0]:
                        if cur is not None:
                            ctx.__exit__(None, None, None)
                            cur = None
                        cur = marks[mi][1]
                        ctx = nc.named_scope(cur)
                        ctx.__enter__()
                        mi += 1
                    fn(e)
                if cur is not None:
                    ctx.__exit__(None, None, None)
            getattr(block, eng)(body)
    return nc


def kernel(**inputs):
    in_maps = host_prepare(**inputs)
    nc = make_nc()
    res = run_bass_kernel_spmd(nc, in_maps, core_ids=list(range(8)))
    outp = np.zeros((B, S, E), np.float32)
    for c in range(8):
        b, q0 = c // 2, TQ * (c % 2)
        outp[b, q0:q0 + TQ] = res.results[c]["out"]
    return outp


# version 04
# speedup vs baseline: 1.0360x; 1.0360x over previous
import math
from collections import defaultdict

import numpy as np
import ml_dtypes

import concourse.bass as bass
import concourse.mybir as mybir
from concourse.bass_utils import run_bass_kernel_spmd

F32 = mybir.dt.float32
BF16 = mybir.dt.bfloat16
U16 = mybir.dt.uint16
ALU = mybir.AluOpType
AFT = mybir.ActivationFunctionType
AX = mybir.AxisListType

B, S, E, H, D, M = 4, 4096, 2048, 16, 128, 16384
TQ = 2048
EC = E // 128
BFNP = np.dtype(ml_dtypes.bfloat16)
IDM = list(range(32))

ENGS = ("sync", "tensor", "vector", "scalar", "gpsimd")

PHASES = 5
NSUP = 16


class KB:

    def __init__(self, nc):
        self.nc = nc
        self.ops = {e: [] for e in ENGS}
        self.cnt = defaultdict(int)
        self.sems = {}
        self.marks = {e: [] for e in ENGS}

    def phase(self, name):
        for e in ENGS:
            self.marks[e].append((len(self.ops[e]), name))

    def sem(self, name):
        if name not in self.sems:
            self.sems[name] = self.nc.semaphore("sem_" + name).__enter__()
        return self.sems[name]

    def on(self, eng, fn):
        self.ops[eng].append(fn)

    def wait(self, eng, sem_name, val):
        if val <= 0:
            return
        s = self.sem(sem_name)
        self.on(eng, lambda e, s=s, v=val: e.wait_ge(s, v))

    def dma(self, eng, out, in_, **kw):
        name = eng + "d"
        s = self.sem(name)
        self.cnt[name] += 16
        v = self.cnt[name]
        self.on(eng, lambda e, o=out, i=in_, kw=dict(kw), s=s:
                e.dma_start(out=o, in_=i, **kw).then_inc(s, 16))
        return v

    def op(self, eng, meth, *args, **kw):
        self.on(eng, lambda e, m=meth, a=args, k=dict(kw): getattr(e, m)(*a, **k))

    def opi(self, eng, meth, *args, **kw):
        s = self.sem(eng)
        self.cnt[eng] += 1
        v = self.cnt[eng]
        self.on(eng, lambda e, m=meth, a=args, k=dict(kw), s=s:
                getattr(e, m)(*a, **k).then_inc(s, 1))
        return v

    def tiny(self, eng, meth, *args, **kw):
        v = self.opi(eng, meth, *args, **kw)
        self.wait(eng, eng, v)
        return v

    def barrier(self):
        snap = [(n, self.cnt[n]) for n in list(self.cnt) if self.cnt[n] > 0]
        for e in ENGS:
            for n, v in snap:
                if n == e:
                    continue
                self.wait(e, n, v)


def build_core_program(nc, dbg=()):
    kb = KB(nc)

    def dram(name, shape, dtype, kind="ExternalInput"):
        if kind == "Internal" and name in dbg:
            kind = "ExternalOutput"
        return nc.dram_tensor(name, list(shape), dtype, kind=kind).ap()

    x = dram("x", (S, E), F32)
    xq = dram("xq", (TQ, E), F32)
    wT = {n: dram(n, (E, E), F32) for n in ("wqT", "wkT", "wvT", "woT")}
    bvecs = {n: dram(n, (1, E), F32) for n in ("bq", "bk", "bv", "bo")}
    gpT = dram("gpT", (E // 4, M), U16)
    upT = dram("upT", (E // 4, M), U16)
    dpT = dram("dpT", (M // 4, E), U16)
    sgT = dram("sgT", (E // 32, M), F32)
    suT = dram("suT", (E // 32, M), F32)
    sdT = dram("sdT", (M // 32, E), F32)
    cos_kv = dram("cos_kv", (D, S), F32)
    sinp_kv = dram("sinp_kv", (D, S), F32)
    cos_q = dram("cos_q", (D, TQ), F32)
    sinp_q = dram("sinp_q", (D, TQ), F32)
    onesb = dram("onesb", (1, 512), BF16)
    hsel_d = dram("hsel", (128, 256), BF16)
    out = dram("out", (TQ, E), F32, kind="ExternalOutput")

    xntm = dram("xntm", (S, E), BF16, kind="Internal")
    xnqtm = dram("xnqtm", (TQ, E), BF16, kind="Internal")
    xnfm = dram("xnfm", (E, S), BF16, kind="Internal")
    xnqfm = dram("xnqfm", (E, TQ), BF16, kind="Internal")
    kcache = dram("kcache", (E, S), BF16, kind="Internal")
    vcache = dram("vcache", (S, H * 129), BF16, kind="Internal")
    qcache = dram("qcache", (E, TQ), BF16, kind="Internal")
    mtmp = dram("mtmp", (16, 1), F32, kind="Internal")
    mglob = dram("mglob", (1, 1), F32, kind="Internal")
    avsp = dram("avsp", (TQ, 2064), F32, kind="Internal")
    avn = dram("avn", (TQ, E), BF16, kind="Internal")
    x1sp = dram("x1sp", (TQ, E), F32, kind="Internal")
    xn2tm = dram("xn2tm", (TQ, E), BF16, kind="Internal")
    xn2fm = dram("xn2fm", (E, TQ), BF16, kind="Internal")
    acc = dram("acc", (TQ, E), F32, kind="Internal")
    sge = dram("sge", (E // 4, M), BF16, kind="Internal")
    sue = dram("sue", (E // 4, M), BF16, kind="Internal")
    sde = dram("sde", (M // 4, E), BF16, kind="Internal")
    wgdump = dram("wgdump", (128, 16384), BF16, kind="Internal")
    wddump = dram("wddump", (128, 16384), BF16, kind="Internal")
    itdump = dram("itdump", (128, 16384), BF16, kind="Internal")

    A = nc.sbuf_tensor("A", [128, 73728], BF16).__enter__()
    U = nc.sbuf_tensor("U", [128, 12288], U16).__enter__()
    Fa = nc.sbuf_tensor("Fa", [128, 4096], F32).__enter__()
    Tb = nc.sbuf_tensor("Tb", [128, 2048], F32).__enter__()
    ST = nc.sbuf_tensor("ST", [128, 32], F32).__enter__()
    epscol = nc.sbuf_tensor("epscol", [128, 1], F32).__enter__()
    lncol = nc.sbuf_tensor("lncol", [1, 1], F32).__enter__()
    negm = nc.sbuf_tensor("negm", [128, 1], F32).__enter__()
    onesr = nc.sbuf_tensor("onesr", [1, 512], BF16).__enter__()
    bias_sb = nc.sbuf_tensor("bias_sb", [1, 2048], BF16).__enter__()
    hsel = nc.sbuf_tensor("hsel_sb", [128, 256], BF16).__enter__()
    m16 = nc.sbuf_tensor("m16", [16, 8], F32).__enter__()
    m1 = nc.sbuf_tensor("m1", [1, 32], F32).__enter__()

    ps_a = nc.psum_tensor("ps_a", [128, 2048], F32).__enter__()
    ps_b = nc.psum_tensor("ps_b", [128, 1536], F32).__enter__()

    def a_(off, n):
        return A[:, off:off + n]

    kb.opi("vector", "memset", epscol[:], 1e-5)
    kb.opi("vector", "memset", lncol[:], -0.5 * math.log(float(D)))
    kb.opi("vector", "memset", m16[:, 0:2], 0.0)
    kb.dma("sync", onesr[:], onesb)
    kb.dma("sync", hsel[:], hsel_d)
    kb.barrier()

    kb.phase("p0_ln1")
    def ln_pass(src, n_tiles, dst_tm):
        junk = a_(0, 2048)
        norm_done = {0: 0, 1: 0}
        store_done = {0: 0, 1: 0}
        for i in range(n_tiles):
            buf = i % 2
            xtile = Fa[:, buf * 2048:(buf + 1) * 2048]
            if i >= 2:
                kb.wait("sync", "scalar", norm_done[buf])
            xv = kb.dma("sync", xtile, src[i * 128:(i + 1) * 128, :])
            kb.wait("scalar", "syncd", xv)
            s1, s2, mean, var, t0, inv, nmi = (ST[:, j:j + 1] for j in range(7))
            kb.op("scalar", "activation", junk, xtile, AFT.Identity, accum_out=s1)
            v_sc = kb.opi("scalar", "activation", junk, xtile, AFT.Square, accum_out=s2)
            kb.wait("vector", "scalar", v_sc)
            kb.tiny("vector", "tensor_scalar", mean, s1, 1.0 / E, None, ALU.mult)
            kb.tiny("vector", "tensor_scalar", var, s2, 1.0 / E, None, ALU.mult)
            kb.tiny("vector", "tensor_tensor", t0, mean, mean, ALU.mult)
            v_ve = kb.tiny("vector", "tensor_sub", var, var, t0)
            kb.wait("scalar", "vector", v_ve)
            kb.tiny("scalar", "activation", t0, var, AFT.Ln, bias=epscol[:])
            v_inv = kb.tiny("scalar", "activation", inv, t0, AFT.Exp, scale=-0.5)
            kb.wait("vector", "scalar", v_inv)
            kb.tiny("vector", "tensor_scalar", nmi, mean, -1.0, None, ALU.mult)
            v_nmi = kb.tiny("vector", "tensor_tensor", nmi, nmi, inv, ALU.mult)
            kb.wait("scalar", "vector", v_nmi)
            xn_tm = a_(2048 + buf * 2048, 2048)
            if i >= 2:
                kb.wait("scalar", "scalard", store_done[buf])
            norm_done[buf] = kb.opi("scalar", "activation", xn_tm, xtile,
                                    AFT.Identity, bias=nmi, scale=inv)
            kb.wait("scalar", "scalar", norm_done[buf])
            store_done[buf] = kb.dma(
                "scalar", dst_tm[i * 128:(i + 1) * 128, :], xn_tm)

    def fm_transpose(src_tm, dst_fm, ncols):
        st_done = {0: 0, 1: 0}
        for ec in range(EC):
            buf = ec % 2
            tile = a_(buf * 8192, ncols)
            if ec >= 2:
                kb.wait("sync", "syncd", st_done[buf])
            v_tr = kb.dma("sync", tile, src_tm[:, ec * 128:(ec + 1) * 128],
                          transpose=True)
            kb.wait("sync", "syncd", v_tr)
            st_done[buf] = kb.dma("sync", dst_fm[ec * 128:(ec + 1) * 128, :], tile)

    if PHASES >= 0:
        ln_pass(x, S // 128, xntm)
        kb.barrier()
        ln_pass(xq, TQ // 128, xnqtm)
        kb.barrier()
        fm_transpose(xntm, xnfm, S)
        fm_transpose(xnqtm, xnqfm, TQ)
        kb.barrier()

    def proj_kq(wname, bname, kind):
        WT_OFF = 16384
        wv = 0
        for ec in range(EC):
            wv = kb.dma("gpsimd", a_(WT_OFF + ec * 2048, 2048),
                        wT[wname][ec * 128:(ec + 1) * 128, :])
        bv_ = kb.dma("gpsimd", bias_sb[:], bvecs[bname])
        kb.wait("tensor", "gpsimdd", wv)
        kb.wait("tensor", "gpsimdd", bv_)
        src_fm = xnqfm if kind == "q" else xnfm
        n_tc = (TQ if kind == "q" else S) // 512
        cos_d = cos_q if kind == "q" else cos_kv
        sin_d = sinp_q if kind == "q" else sinp_kv
        dstc = qcache if kind == "q" else kcache
        xn_used = {0: 0, 1: 0}
        tab_used = {0: (0, 0), 1: (0, 0)}
        pk_free = {0: 0, 1: 0}
        kt_free = {0: 0, 1: 0}
        nrm_free = 0
        for tc in range(n_tc):
            buf = tc % 2
            if tc >= 2:
                kb.wait("sync", "tensor", xn_used[buf])
                kb.wait("sync", "vector", tab_used[buf][0])
                kb.wait("sync", "gpsimd", tab_used[buf][1])
            xv = 0
            for ec in range(EC):
                xv = kb.dma("sync", a_(buf * 8192 + ec * 512, 512),
                            src_fm[ec * 128:(ec + 1) * 128, tc * 512:(tc + 1) * 512])
            kb.dma("sync", Tb[:, buf * 1024:buf * 1024 + 512],
                   cos_d[:, tc * 512:(tc + 1) * 512])
            tv = kb.dma("sync", Tb[:, buf * 1024 + 512:buf * 1024 + 1024],
                        sin_d[:, tc * 512:(tc + 1) * 512])
            kb.wait("tensor", "syncd", xv)
            kb.wait("vector", "syncd", tv)
            kb.wait("tensor", "vector", nrm_free)
            for ds in range(EC):
                pbuf = (tc * EC + ds) % 2
                pk = ps_b[:, pbuf * 512:(pbuf + 1) * 512]
                kb.wait("tensor", "vector", pk_free[pbuf])
                for ec in range(EC):
                    kb.op("tensor", "matmul", pk,
                          a_(WT_OFF + ec * 2048 + ds * 128, 128),
                          a_(buf * 8192 + ec * 512, 512),
                          start=(ec == 0), stop=False)
                mv = kb.opi("tensor", "matmul", pk,
                            bias_sb[:, ds * 128:ds * 128 + 128], onesr[:],
                            start=False, stop=True)
                kb.wait("vector", "tensor", mv)
                qsht = Fa[:, pbuf * 512:(pbuf + 1) * 512]
                kb.op("vector", "stream_shuffle", qsht[0:64, :], pk[64:128, :], IDM)
                v_sh = kb.opi("vector", "stream_shuffle", qsht[64:128, :],
                              pk[0:64, :], IDM)
                t1 = a_(49152 + pbuf * 512, 512)
                v_t1 = kb.opi("vector", "tensor_tensor", t1, pk,
                              Tb[:, buf * 1024:buf * 1024 + 512], ALU.mult)
                pk_free[pbuf] = v_t1
                t2 = a_(50176 + pbuf * 512, 512)
                kb.wait("gpsimd", "vector", v_sh)
                kb.op("gpsimd", "tensor_tensor", t2, qsht,
                      Tb[:, buf * 1024 + 512:buf * 1024 + 1024], ALU.mult)
                ktile = a_(51200 + pbuf * 512, 512)
                kb.wait("gpsimd", "vector", v_t1)
                if kt_free[pbuf]:
                    kb.wait("gpsimd", "gpsimdd", kt_free[pbuf])
                v_kt = kb.opi("gpsimd", "tensor_tensor", ktile, t1, t2, ALU.add)
                kb.wait("gpsimd", "gpsimd", v_kt)
                kt_free[pbuf] = kb.dma(
                    "gpsimd", dstc[ds * 128:(ds + 1) * 128, tc * 512:(tc + 1) * 512],
                    ktile)
                sq = a_(52224 + pbuf * 512, 512)
                kb.wait("scalar", "gpsimd", v_kt)
                v_sq = kb.opi("scalar", "activation", sq, ktile, AFT.Square)
                kb.wait("tensor", "scalar", v_sq)
                v_mm = kb.opi("tensor", "matmul", ps_b[0:16, 1024:1536],
                              hsel[:, ds * 16:ds * 16 + 16], sq,
                              start=(ds == 0), stop=(ds == EC - 1),
                              skip_group_check=True)
            xn_used[buf] = v_mm
            tab_used[buf] = (kb.cnt["vector"], kb.cnt["gpsimd"])
            col = 0 if kind == "k" else 1
            kb.wait("vector", "tensor", v_mm)
            kb.tiny("vector", "reduce_max", m16[:, 2:3], ps_b[0:16, 1024:1536], AX.X)
            nrm_free = kb.tiny("vector", "tensor_max", m16[:, col:col + 1],
                               m16[:, col:col + 1], m16[:, 2:3])

    def proj_v():
        WT_OFF = 16384
        wv = 0
        for ec in range(EC):
            wv = kb.dma("gpsimd", a_(WT_OFF + ec * 2048, 2048),
                        wT["wvT"][ec * 128:(ec + 1) * 128, :])
        bv_ = kb.dma("gpsimd", bias_sb[:], bvecs["bv"])
        kb.wait("tensor", "gpsimdd", wv)
        kb.wait("tensor", "gpsimdd", bv_)
        xn_used = {0: 0, 1: 0}
        pv_free = {0: 0, 1: 0}
        vt_free = {0: 0, 1: 0}
        for tc in range(S // 512):
            buf = tc % 2
            if tc >= 2:
                kb.wait("sync", "tensor", xn_used[buf])
            xv = 0
            for ec in range(EC):
                xv = kb.dma("sync", a_(buf * 8192 + ec * 512, 512),
                            xnfm[ec * 128:(ec + 1) * 128, tc * 512:(tc + 1) * 512])
            kb.wait("tensor", "syncd", xv)
            for st_i in range(4):
                vtb = (tc * 4 + st_i) % 2
                vtile = a_(49152 + vtb * 2064, 2064)
                vt_r = vtile.rearrange("p (h c) -> p h c", h=16)
                if vt_free[vtb]:
                    kb.wait("vector", "scalard", vt_free[vtb])
                v_ms = kb.opi("vector", "memset", vt_r[:, :, 128:129], 1.0)
                for dc in range(4):
                    pbuf = (tc * 16 + st_i * 4 + dc) % 2
                    pv = ps_b[:, pbuf * 512:(pbuf + 1) * 512]
                    kb.wait("tensor", "scalar", pv_free[pbuf])
                    for ec in range(EC):
                        kb.op("tensor", "matmul", pv,
                              a_(buf * 8192 + ec * 512 + st_i * 128, 128),
                              a_(WT_OFF + ec * 2048 + dc * 512, 512),
                              start=(ec == 0), stop=False)
                    mv = kb.opi("tensor", "matmul", pv, onesr[:, 0:128],
                                bias_sb[:, dc * 512:(dc + 1) * 512],
                                start=False, stop=True)
                    kb.wait("scalar", "tensor", mv)
                    if dc == 0:
                        kb.wait("scalar", "vector", v_ms)
                    pv_free[pbuf] = kb.opi(
                        "scalar", "activation",
                        vt_r[:, dc * 4:dc * 4 + 4, 0:128],
                        pv.rearrange("p (a c) -> p a c", a=4), AFT.Identity)
                kb.wait("scalar", "scalar", kb.cnt["scalar"])
                vt_free[vtb] = kb.dma(
                    "scalar",
                    vcache[tc * 512 + st_i * 128: tc * 512 + (st_i + 1) * 128, :],
                    vtile)
            xn_used[buf] = kb.cnt["tensor"]

    if PHASES >= 1:
        kb.phase("p1k")
        proj_kq("wkT", "bk", "k")
        kb.barrier()
        kb.phase("p1v")
        proj_v()
        kb.barrier()
        kb.phase("p1q")
        proj_kq("wqT", "bq", "q")
        kb.barrier()
        kb.tiny("vector", "tensor_tensor", m16[:, 3:4], m16[:, 0:1], m16[:, 1:2],
                ALU.mult)
        kb.wait("gpsimd", "vector", kb.cnt["vector"])
        mv1 = kb.dma("gpsimd", mtmp, m16[:, 3:4])
        kb.wait("sync", "gpsimdd", mv1)
        mv2 = kb.dma("sync", m1[:, 0:16], mtmp.rearrange("a b -> b a"))
        kb.wait("vector", "syncd", mv2)
        v_mx = kb.tiny("vector", "reduce_max", m1[:, 16:17], m1[:, 0:16], AX.X)
        kb.wait("scalar", "vector", v_mx)
        kb.tiny("scalar", "activation", m1[:, 17:18], m1[:, 16:17], AFT.Ln)
        v_m = kb.tiny("scalar", "activation", m1[:, 18:19], m1[:, 17:18], AFT.Exp,
                      scale=0.5, bias=lncol[:])
        kb.wait("vector", "scalar", v_m)
        v_neg = kb.opi("vector", "tensor_scalar", m1[:, 19:20], m1[:, 18:19],
                       -1.0, None, ALU.mult)
        kb.wait("gpsimd", "vector", v_neg)
        mv3 = kb.dma("gpsimd", mglob, m1[:, 19:20])
        kb.wait("sync", "gpsimdd", mv3)
        kb.dma("sync", negm[:], mglob.broadcast_to([128, 1]))
        kb.barrier()

    SM = nc.sbuf_tensor("SM", [128, 48], F32).__enter__()
    if PHASES >= 2:
        kb.phase("p2_attn")
        for half in (0, 1):
            kv = kb.dma("sync", A[:, 0:32768].rearrange("d (h s) -> d h s", h=16),
                        kcache.rearrange("(h d) s -> d h s", d=128)
                        [:, :, half * 2048:(half + 1) * 2048])
            kv2 = kb.dma("sync", A[:, 32768:65792].rearrange("p (a c) -> p a c", a=16),
                         vcache.rearrange("(a p) c -> p a c", p=128)
                         [:, half * 16:(half + 1) * 16, :])
            kb.wait("tensor", "syncd", kv)
            kb.wait("tensor", "syncd", kv2)
            qsb_used = {0: 0, 1: 0}
            sc_free = {0: 0, 1: 0}
            p_free = {0: 0, 1: 0}
            av_free_s = 0
            av_free_v = 0
            sp_used = 0
            avn_used = 0
            spv = 0
            for qb in range(16):
                qbuf = qb % 2
                if qb >= 2:
                    kb.wait("sync", "tensor", qsb_used[qbuf])
                qv = kb.dma("sync",
                            A[:, 65792 + qbuf * 2048:65792 + (qbuf + 1) * 2048]
                            .rearrange("d (h t) -> d h t", h=16),
                            qcache.rearrange("(h d) t -> d h t", d=128)
                            [:, :, qb * 128:(qb + 1) * 128])
                kb.wait("tensor", "syncd", qv)
                if half == 1:
                    kb.wait("sync", "vector", sp_used)
                    spv = kb.dma("sync", Fa[:, 0:2064],
                                 avsp[qb * 128:(qb + 1) * 128, :])
                kb.wait("tensor", "scalar", av_free_s)
                kb.wait("tensor", "vector", av_free_v)
                v_last = 0
                iters = [(h, grp) for h in range(16) for grp in range(4)]

                def emit_scores(k):
                    h, grp = iters[k]
                    pb = k % 2
                    sc = ps_b[:, 512 * (1 + pb):512 * (2 + pb)]
                    kb.wait("tensor", "scalar", sc_free[pb])
                    for ss in range(4):
                        emit = kb.opi if ss == 3 else kb.op
                        r = emit("tensor", "matmul",
                                 sc[:, ss * 128:(ss + 1) * 128],
                                 a_(h * 2048 + grp * 512 + ss * 128, 128),
                                 a_(65792 + qbuf * 2048 + h * 128, 128),
                                 start=True, stop=True)
                    return r

                def emit_exp(k, v_scmm):
                    pb = k % 2
                    sc = ps_b[:, 512 * (1 + pb):512 * (2 + pb)]
                    pgrp = a_(69888 + pb * 512, 512)
                    kb.wait("scalar", "tensor", max(v_scmm, p_free[pb]))
                    sc_free[pb] = kb.opi("scalar", "activation", pgrp, sc,
                                         AFT.Exp,
                                         scale=float(1.0 / math.sqrt(D)),
                                         bias=negm[:])
                    return sc_free[pb]

                def emit_av(k):
                    h, grp = iters[k]
                    pb = k % 2
                    pgrp = a_(69888 + pb * 512, 512)
                    kb.wait("tensor", "scalar", sc_free[pb])
                    for ss in range(4):
                        st_idx = grp * 4 + ss
                        vb = 32768 + st_idx * 2064 + 129 * h
                        kb.op("tensor", "matmul",
                              ps_a[:, h * 128:(h + 1) * 128],
                              pgrp[:, ss * 128:(ss + 1) * 128],
                              A[:, vb:vb + 128],
                              start=(ss == 0 and grp == 0 and (h % 4) == 0),
                              stop=False, skip_group_check=True)
                        p_free[pb] = kb.opi(
                            "tensor", "matmul", ps_b[:, h:h + 1],
                            pgrp[:, ss * 128:(ss + 1) * 128],
                            A[:, vb + 128:vb + 129],
                            start=(ss == 0 and grp == 0 and h == 0),
                            stop=False, skip_group_check=True)
                    return p_free[pb]

                v_sc_prev = emit_scores(0)
                emit_exp(0, v_sc_prev)
                for k in range(1, 64):
                    v_sc_k = emit_scores(k)
                    emit_exp(k, v_sc_k)
                    v_last = emit_av(k - 1)
                v_last = emit_av(63)
                if half == 0:
                    kb.wait("scalar", "tensor", v_last)
                    kb.op("scalar", "activation", Fa[:, 0:2048], ps_a[:], AFT.Identity)
                    av_free_s = kb.opi("scalar", "activation", Fa[:, 2048:2064],
                                       ps_b[:, 0:16], AFT.Identity)
                    kb.wait("scalar", "scalar", av_free_s)
                    kb.dma("scalar", avsp[qb * 128:(qb + 1) * 128, :], Fa[:, 0:2064])
                else:
                    kb.wait("vector", "tensor", v_last)
                    kb.wait("vector", "syncd", spv)
                    kb.tiny("vector", "tensor_add", SM[:, 0:16], ps_b[:, 0:16],
                            Fa[:, 2048:2064])
                    kb.tiny("vector", "reciprocal", SM[:, 16:32], SM[:, 0:16])
                    av_free_v = kb.opi("vector", "tensor_add", Fa[:, 0:2048],
                                       ps_a[:], Fa[:, 0:2048])
                    avn_sb = a_(70912, 2048)
                    if avn_used:
                        kb.wait("vector", "gpsimdd", avn_used)
                    v_mul = 0
                    for h in range(16):
                        emit = kb.opi if h == 15 else kb.op
                        r = emit("vector", "tensor_tensor",
                                 avn_sb[:, h * 128:(h + 1) * 128],
                                 Fa[:, h * 128:(h + 1) * 128],
                                 SM[:, 16 + h:17 + h].broadcast_to([128, 128]),
                                 ALU.mult)
                        if h == 15:
                            v_mul = r
                    sp_used = v_mul
                    kb.wait("gpsimd", "vector", v_mul)
                    avn_used = kb.dma("gpsimd", avn[qb * 128:(qb + 1) * 128, :],
                                      avn_sb)
                qsb_used[qbuf] = kb.cnt["tensor"]
            kb.barrier()

    if PHASES >= 3:
        kb.phase("p25_oproj")
        wv = 0
        for ec in range(EC):
            wv = kb.dma("gpsimd", a_(ec * 2048, 2048),
                        wT["woT"][ec * 128:(ec + 1) * 128, :])
        bv_ = kb.dma("gpsimd", bias_sb[:], bvecs["bo"])
        kb.wait("tensor", "gpsimdd", wv)
        kb.wait("tensor", "gpsimdd", bv_)
        junk2 = a_(34816, 512)
        xn2_used_c = 0
        v_x1_prev = 0
        avt_used = 0
        x1_used = 0
        xn2_used = 0
        po_free = {0: 0, 1: 0}
        for qb in range(16):
            avv = 0
            for ec in range(EC):
                if ec < 2:
                    kb.wait("sync", "tensor", avt_used)
                kb.dma("sync", a_(32768 + ec * 128, 128),
                       avn[qb * 128:(qb + 1) * 128, ec * 128:(ec + 1) * 128],
                       transpose=True)
                avv = kb.cnt["syncd"]
            kb.wait("sync", "scalar", xn2_used_c)
            xqv = kb.dma("sync", Fa[:, 2048:4096], xq[qb * 128:(qb + 1) * 128, :])
            kb.wait("tensor", "syncd", avv)
            kb.wait("scalar", "vector", v_x1_prev)
            for eoc in range(4):
                pb = eoc % 2
                po = ps_b[:, pb * 512:(pb + 1) * 512]
                kb.wait("tensor", "scalar", po_free[pb])
                for ec in range(EC):
                    kb.op("tensor", "matmul", po, a_(32768 + ec * 128, 128),
                          a_(ec * 2048 + eoc * 512, 512),
                          start=(ec == 0), stop=False)
                mv = kb.opi("tensor", "matmul", po, onesr[:, 0:128],
                            bias_sb[:, eoc * 512:(eoc + 1) * 512],
                            start=False, stop=True)
                kb.wait("scalar", "tensor", mv)
                kb.op("scalar", "activation", Fa[:, eoc * 512:(eoc + 1) * 512],
                      po, AFT.Identity, accum_out=ST[:, 8 + eoc:9 + eoc])
                po_free[pb] = kb.opi("scalar", "activation", junk2, po,
                                     AFT.Square, accum_out=ST[:, 12 + eoc:13 + eoc])
            avt_used = mv
            s1, s2, mean, var, t0, inv, nmi = (ST[:, j:j + 1] for j in range(7))
            kb.wait("vector", "scalar", po_free[pb])
            kb.tiny("vector", "reduce_sum", s1, ST[:, 8:12], AX.X)
            kb.tiny("vector", "reduce_sum", s2, ST[:, 12:16], AX.X)
            kb.tiny("vector", "tensor_scalar", mean, s1, 1.0 / E, None, ALU.mult)
            kb.tiny("vector", "tensor_scalar", var, s2, 1.0 / E, None, ALU.mult)
            kb.tiny("vector", "tensor_tensor", t0, mean, mean, ALU.mult)
            v_ve = kb.tiny("vector", "tensor_sub", var, var, t0)
            kb.wait("scalar", "vector", v_ve)
            kb.tiny("scalar", "activation", t0, var, AFT.Ln, bias=epscol[:])
            v_inv = kb.tiny("scalar", "activation", inv, t0, AFT.Exp, scale=-0.5)
            kb.wait("vector", "scalar", v_inv)
            kb.tiny("vector", "tensor_scalar", nmi, mean, -1.0, None, ALU.mult)
            v_nmi = kb.tiny("vector", "tensor_tensor", nmi, nmi, inv, ALU.mult)
            kb.wait("scalar", "vector", v_nmi)
            v_n2 = kb.opi("scalar", "activation", Fa[:, 0:2048], Fa[:, 0:2048],
                          AFT.Identity, bias=nmi, scale=inv)
            kb.wait("vector", "scalar", v_n2)
            kb.wait("vector", "syncd", xqv)
            if x1_used:
                kb.wait("vector", "gpsimdd", x1_used)
            v_x1 = kb.opi("vector", "tensor_add", Fa[:, 2048:4096],
                          Fa[:, 2048:4096], Fa[:, 0:2048])
            kb.wait("gpsimd", "vector", v_x1)
            x1_used = kb.dma("gpsimd", x1sp[qb * 128:(qb + 1) * 128, :],
                             Fa[:, 2048:4096])
            kb.wait("scalar", "vector", v_x1)
            for c4 in range(4):
                kb.op("scalar", "activation", junk2,
                      Fa[:, 2048 + c4 * 512:2048 + (c4 + 1) * 512],
                      AFT.Identity, accum_out=ST[:, 8 + c4:9 + c4])
            for c4 in range(4):
                r = (kb.opi if c4 == 3 else kb.op)(
                    "scalar", "activation", junk2,
                    Fa[:, 2048 + c4 * 512:2048 + (c4 + 1) * 512],
                    AFT.Square, accum_out=ST[:, 12 + c4:13 + c4])
            v_st3 = r
            kb.wait("vector", "scalar", v_st3)
            kb.tiny("vector", "reduce_sum", s1, ST[:, 8:12], AX.X)
            kb.tiny("vector", "reduce_sum", s2, ST[:, 12:16], AX.X)
            kb.tiny("vector", "tensor_scalar", mean, s1, 1.0 / E, None, ALU.mult)
            kb.tiny("vector", "tensor_scalar", var, s2, 1.0 / E, None, ALU.mult)
            kb.tiny("vector", "tensor_tensor", t0, mean, mean, ALU.mult)
            v_ve = kb.tiny("vector", "tensor_sub", var, var, t0)
            kb.wait("scalar", "vector", v_ve)
            kb.tiny("scalar", "activation", t0, var, AFT.Ln, bias=epscol[:])
            v_inv = kb.tiny("scalar", "activation", inv, t0, AFT.Exp, scale=-0.5)
            kb.wait("vector", "scalar", v_inv)
            kb.tiny("vector", "tensor_scalar", nmi, mean, -1.0, None, ALU.mult)
            v_nmi = kb.tiny("vector", "tensor_tensor", nmi, nmi, inv, ALU.mult)
            kb.wait("scalar", "vector", v_nmi)
            xn2sb = a_(35328, 2048)
            if xn2_used:
                kb.wait("scalar", "scalard", xn2_used)
            v_xn2 = kb.opi("scalar", "activation", xn2sb, Fa[:, 2048:4096],
                           AFT.Identity, bias=nmi, scale=inv)
            xn2_used_c = v_xn2
            v_x1_prev = v_x1
            kb.wait("scalar", "scalar", v_xn2)
            xn2_used = kb.dma("scalar", xn2tm[qb * 128:(qb + 1) * 128, :], xn2sb)
        kb.barrier()

    def nib_ops(r):
        masks = [(0x0007, 6, ALU.logical_shift_left, 0x0008, 12),
                 (0x0070, 2, ALU.logical_shift_left, 0x0080, 8),
                 (0x0700, 2, ALU.logical_shift_right, 0x0800, 4),
                 (0x7000, 6, ALU.logical_shift_right, 0x8000, 0)]
        return masks[r]

    if PHASES >= 4:
        kb.phase("p3_scaleprep")
        P126 = float(2.0 ** 126)
        for srcT, dstE, nrow in ((sgT, sge, 64), (suT, sue, 64), (sdT, sde, 128)):
            nchunk = 4
            for ci in range(nchunk):
                if nrow == 64:
                    src_ap = srcT[:, ci * 4096:(ci + 1) * 4096]
                    ftile = Fa[0:64, :]
                    btile = a_(0, 4096)[0:64, :]
                else:
                    src_ap = srcT[ci * 128:(ci + 1) * 128, :]
                    ftile = Fa[:, 0:2048]
                    btile = a_(0, 2048)
                kb.wait("sync", "vector", kb.cnt["vector"])
                lv = kb.dma("sync", ftile, src_ap)
                kb.wait("vector", "syncd", lv)
                v_m2 = kb.opi("vector", "tensor_scalar", btile, ftile, P126,
                              None, ALU.mult)
                kb.wait("gpsimd", "vector", v_m2)
                stv = 0
                for e8 in range(8):
                    if nrow == 64:
                        dst_ap = dstE.rearrange("(a e) c -> a e c", e=8)[
                            :, e8, ci * 4096:(ci + 1) * 4096]
                    else:
                        dst_ap = dstE.rearrange("(a e) c -> a e c", e=8)[
                            ci * 128:(ci + 1) * 128, e8, :]
                    stv = kb.dma("gpsimd", dst_ap, btile)
                kb.wait("vector", "gpsimdd", stv)
        kb.barrier()
        fm_transpose(xn2tm, xn2fm, TQ)
        kb.barrier()
        for bb in range(EC):
            rstart = 512 * (bb % 4) + bb // 4
            v_t = kb.dma("sync", a_(bb * 2048, 2048),
                         xn2fm[rstart:rstart + 509:4, :])
        kb.wait("tensor", "syncd", v_t)
        kb.barrier()

        kb.phase("p3_ffn")
        R1 = 32768
        ITR = 49152
        SCG, SCU, SCD = 65536, 67584, 69632
        U32 = mybir.dt.uint32
        EM0, EM1, SGT_, GGT = 71680, 72192, 72704, 73216
        em2 = [a_(EM0, 512), a_(EM1, 512)]
        sg32 = a_(SGT_, 512).bitcast(U32)
        gg = a_(GGT, 512)

        def nib32(r):
            msk, sh, shop, msk2, sh2 = nib_ops(r)
            return (msk | (msk << 16), sh, shop, msk2 | (msk2 << 16), sh2)

        vv_w = {"vector": 0, "gpsimd": 0}
        em_free2 = [(0, 0), (0, 0)]

        def dequant(usrc_u16, wslot, scales_ap):
            nonlocal nidx
            usrc = usrc_u16.bitcast(U32)
            for r in range(4):
                msk, sh, shop, msk2, sh2 = nib32(r)
                eb = em2[nidx % 2]
                eb32 = eb.bitcast(U32)
                kb.wait("vector", "vector", em_free2[nidx % 2][0])
                kb.wait("vector", "gpsimd", em_free2[nidx % 2][1])
                kb.op("vector", "tensor_scalar", eb32, usrc,
                      msk, sh, ALU.bitwise_and, shop)
                if sh2:
                    kb.op("vector", "tensor_scalar", sg32, usrc,
                          msk2, sh2, ALU.bitwise_and, ALU.logical_shift_left)
                else:
                    kb.op("vector", "tensor_scalar", sg32, usrc,
                          msk2, None, ALU.bitwise_and)
                v_or = kb.opi("vector", "tensor_tensor", eb32, eb32,
                              sg32, ALU.bitwise_or)
                eng = "gpsimd" if nidx % 2 == 0 else "vector"
                if eng == "gpsimd":
                    kb.wait("gpsimd", "vector", v_or)
                vv_w[eng] = kb.opi(eng, "tensor_tensor", a_(wslot(r), 512),
                                   eb.bitcast(BF16), scales_ap, ALU.mult)
                em_free2[nidx % 2] = ((vv_w["vector"], kb.cnt["gpsimd"])
                                      if eng == "vector" else
                                      (kb.cnt["vector"], vv_w["gpsimd"]))
                nidx += 1
        acc_dma = [0] * 16
        em_free = {0: 0, 1: 0}
        u_free = {0: 0, 1: 0}
        ud_free = {0: 0, 1: 0}
        nidx = 0
        v_gu_last = 0
        gg_free = 0
        dq_free = 0
        itr_free = 0
        dbuf_free = {0: 0, 1: 0}
        psd_free = {0: 0, 1: 0}
        psg_free = {0: 0, 1: 0}
        psu_free = {0: 0, 1: 0}
        scl_free = 0
        scl_free_v = 0
        for sup in range(NSUP):
            for cpar in range(2):
                j = sup * 2 + cpar
                ub = j % 2
                kb.wait("sync", "vector", u_free[ub])
                pv = 0
                for a4 in range(4):
                    pv = kb.dma("sync", U[:, ub * 2048 + a4 * 512:ub * 2048 + (a4 + 1) * 512],
                                gpT[a4 * 128:(a4 + 1) * 128, j * 512:(j + 1) * 512])
                    pv = kb.dma("sync", U[:, 4096 + ub * 2048 + a4 * 512:4096 + ub * 2048 + (a4 + 1) * 512],
                                upT[a4 * 128:(a4 + 1) * 128, j * 512:(j + 1) * 512])
                kb.wait("sync", "gpsimd", scl_free)
                kb.wait("sync", "vector", scl_free_v)
                sv = 0
                for a4 in range(4):
                    sv = kb.dma("sync", a_(SCG + a4 * 512, 512),
                                sge[a4 * 128:(a4 + 1) * 128, j * 512:(j + 1) * 512])
                    sv = kb.dma("sync", a_(SCU + a4 * 512, 512),
                                sue[a4 * 128:(a4 + 1) * 128, j * 512:(j + 1) * 512])
                kb.wait("vector", "syncd", pv)
                kb.wait("vector", "syncd", sv)
                kb.wait("gpsimd", "syncd", pv)
                kb.wait("gpsimd", "syncd", sv)
                if cpar == 0:
                    kb.wait("vector", "tensor", dq_free)
                    kb.wait("gpsimd", "tensor", dq_free)
                for mi, (uoff, woff, scoff) in enumerate(
                        ((0, R1, SCG), (4096, R1 + 8192, SCU))):
                    for a4 in range(4):
                        usrc = U[:, uoff + ub * 2048 + a4 * 512:uoff + ub * 2048 + (a4 + 1) * 512]
                        dequant(usrc,
                                (lambda r, a4=a4, woff=woff: woff + (4 * r + a4) * 512),
                                a_(scoff + a4 * 512, 512))
                v_w_v, v_w_g = vv_w["vector"], vv_w["gpsimd"]
                scl_free = v_w_g
                scl_free_v = v_w_v
                u_free[ub] = kb.cnt["vector"]
                if j == 0 and mi == 1 and "wgdump" in dbg:
                    kb.wait("gpsimd", "gpsimd", v_w)
                    kb.dma("gpsimd", wgdump, a_(R1, 16384))

                kb.wait("tensor", "gpsimd", v_w_g)
                kb.wait("tensor", "vector", v_w_v)
                for rn in range(4):
                    for tch in range(4):
                        gpb = tch % 2
                        psg = ps_a[:, gpb * 512:(gpb + 1) * 512]
                        psu = ps_a[:, 1024 + gpb * 512:1024 + (gpb + 1) * 512]
                        kb.wait("tensor", "scalar", psg_free[gpb])
                        kb.wait("tensor", "vector", psu_free[gpb])
                        for bb in range(EC):
                            kb.op("tensor", "matmul", psg,
                                  A[:, R1 + bb * 512 + rn:R1 + (bb + 1) * 512:4],
                                  a_(bb * 2048 + tch * 512, 512),
                                  start=(bb == 0), stop=(bb == EC - 1))
                        v_gu = 0
                        for bb in range(EC):
                            emit = kb.opi if bb == EC - 1 else kb.op
                            r2 = emit("tensor", "matmul", psu,
                                      A[:, R1 + 8192 + bb * 512 + rn:R1 + 8192 + (bb + 1) * 512:4],
                                      a_(bb * 2048 + tch * 512, 512),
                                      start=(bb == 0), stop=(bb == EC - 1))
                            if bb == EC - 1:
                                v_gu = r2
                        v_gu_last = v_gu
                        kb.wait("scalar", "tensor", v_gu)
                        kb.wait("scalar", "vector", gg_free)
                        psg_free[gpb] = kb.opi("scalar", "activation", gg, psg,
                                               AFT.Gelu_apprx_tanh)
                        kb.wait("vector", "scalar", psg_free[gpb])
                        psu_free[gpb] = kb.opi(
                            "vector", "tensor_tensor",
                            a_(ITR + cpar * 8192 + rn * 2048 + tch * 512, 512),
                            psu, gg, ALU.mult)
                        gg_free = psu_free[gpb]
            kb.wait("vector", "tensor", v_gu_last)
            kb.wait("gpsimd", "tensor", v_gu_last)
            for cpar in range(2):
                j = sup * 2 + cpar
                ub = j % 2
                kb.wait("sync", "vector", ud_free[ub])
                kb.wait("sync", "gpsimd", kb.cnt["gpsimd"])
                dv = kb.dma("sync", U[:, 8192 + ub * 2048:8192 + (ub + 1) * 2048],
                            dpT[j * 128:(j + 1) * 128, :])
                sv = kb.dma("sync", a_(SCD, 2048), sde[j * 128:(j + 1) * 128, :])
                kb.wait("vector", "syncd", dv)
                kb.wait("vector", "syncd", sv)
                kb.wait("gpsimd", "syncd", dv)
                kb.wait("gpsimd", "syncd", sv)
                for c4 in range(4):
                    usrc = U[:, 8192 + ub * 2048 + c4 * 512:8192 + ub * 2048 + (c4 + 1) * 512]
                    dequant(usrc,
                            (lambda r, c4=c4, cpar=cpar: R1 + cpar * 8192 + r * 2048 + c4 * 512),
                            a_(SCD + c4 * 512, 512))
                ud_free[ub] = kb.cnt["vector"]
            v_wd_v, v_wd_g = vv_w["vector"], vv_w["gpsimd"]
            kb.wait("tensor", "gpsimd", v_wd_g)
            kb.wait("tensor", "vector", v_wd_v)
            if sup == 0 and "wddump" in dbg:
                kb.wait("gpsimd", "gpsimd", v_wd_g)
                kb.wait("gpsimd", "vector", v_wd_v)
                kb.dma("gpsimd", wddump, a_(R1, 16384))
                kb.wait("gpsimd", "vector", kb.cnt["vector"])
                kb.dma("gpsimd", itdump, a_(ITR, 16384))
            for tsub in range(16):
                dbuf = tsub % 2
                kb.wait("scalar", "gpsimdd", dbuf_free[dbuf])
                for hc in range(4):
                    pdb = hc % 2
                    psd = ps_b[:, pdb * 512:(pdb + 1) * 512]
                    kb.wait("tensor", "scalar", psd_free[pdb])
                    nmm = 0
                    for cpar in range(2):
                        for r in range(4):
                            emit = kb.opi if nmm == 7 else kb.op
                            r3 = emit("tensor", "matmul", psd,
                                      a_(ITR + cpar * 8192 + r * 2048 + tsub * 128, 128),
                                      a_(R1 + cpar * 8192 + r * 2048 + hc * 512, 512),
                                      start=(nmm == 0), stop=(nmm == 7))
                            if nmm == 7:
                                v_dn = r3
                            nmm += 1
                    kb.wait("scalar", "tensor", v_dn)
                    psd_free[pdb] = kb.opi(
                        "scalar", "activation",
                        Fa[:, dbuf * 2048 + hc * 512:dbuf * 2048 + (hc + 1) * 512],
                        psd, AFT.Identity)
                kb.wait("gpsimd", "scalar", psd_free[pdb])
                if acc_dma[tsub]:
                    kb.wait("gpsimd", "gpsimdd", acc_dma[tsub])
                kw = {} if sup == 0 else dict(accum_op=ALU.add)
                acc_dma[tsub] = kb.dma(
                    "gpsimd", acc[tsub * 128:(tsub + 1) * 128, :],
                    Fa[:, dbuf * 2048:(dbuf + 1) * 2048],
                    max_dma_last_dim=8192, **kw)
                dbuf_free[dbuf] = acc_dma[tsub]
            dq_free = v_dn
            itr_free = v_dn
        kb.barrier()

    if PHASES >= 5:
        kb.phase("p4_final")
        junk4 = a_(16384, 2048)
        out_done = {0: 0, 1: 0}
        norm4 = {0: 0, 1: 0}
        addv4 = {0: 0, 1: 0}
        for tsub in range(16):
            buf = tsub % 2
            acct = Fa[:, buf * 2048:(buf + 1) * 2048]
            x1t = a_(buf * 4096, 4096).bitcast(F32)
            outt = a_(8192 + buf * 4096, 4096).bitcast(F32)
            if tsub >= 2:
                kb.wait("sync", "scalar", norm4[buf])
                kb.wait("sync", "vector", addv4[buf])
            av1 = kb.dma("sync", acct, acc[tsub * 128:(tsub + 1) * 128, :])
            xv1 = kb.dma("sync", x1t, x1sp[tsub * 128:(tsub + 1) * 128, :])
            kb.wait("scalar", "syncd", av1)
            s1, s2, mean, var, t0, inv, nmi = (ST[:, j:j + 1] for j in range(7))
            kb.op("scalar", "activation", junk4, acct, AFT.Identity, accum_out=s1)
            v_sc = kb.opi("scalar", "activation", junk4, acct, AFT.Square,
                          accum_out=s2)
            kb.wait("vector", "scalar", v_sc)
            kb.tiny("vector", "tensor_scalar", mean, s1, 1.0 / E, None, ALU.mult)
            kb.tiny("vector", "tensor_scalar", var, s2, 1.0 / E, None, ALU.mult)
            kb.tiny("vector", "tensor_tensor", t0, mean, mean, ALU.mult)
            v_ve = kb.tiny("vector", "tensor_sub", var, var, t0)
            kb.wait("scalar", "vector", v_ve)
            kb.tiny("scalar", "activation", t0, var, AFT.Ln, bias=epscol[:])
            v_inv = kb.tiny("scalar", "activation", inv, t0, AFT.Exp, scale=-0.5)
            kb.wait("vector", "scalar", v_inv)
            kb.tiny("vector", "tensor_scalar", nmi, mean, -1.0, None, ALU.mult)
            v_nmi = kb.tiny("vector", "tensor_tensor", nmi, nmi, inv, ALU.mult)
            kb.wait("scalar", "vector", v_nmi)
            norm4[buf] = kb.opi("scalar", "activation", acct, acct,
                                AFT.Identity, bias=nmi, scale=inv)
            kb.wait("vector", "scalar", norm4[buf])
            kb.wait("vector", "syncd", xv1)
            if tsub >= 2:
                kb.wait("vector", "gpsimdd", out_done[buf])
            v_o = kb.opi("vector", "tensor_add", outt, acct, x1t)
            addv4[buf] = v_o
            kb.wait("gpsimd", "vector", v_o)
            out_done[buf] = kb.dma("gpsimd", out[tsub * 128:(tsub + 1) * 128, :],
                                   outt)
        kb.barrier()

    if PHASES < 5:
        v_o = 0
        for i in range(TQ // 128):
            v_o = kb.dma("sync", out[i * 128:(i + 1) * 128, :],
                         Fa[:, 0:2048])
        kb.wait("sync", "syncd", v_o)

    return kb


def host_prepare(L0_attn_x, L0_attn_Wq, L0_attn_bq, L0_attn_Wk, L0_attn_bk,
                 L0_attn_Wv, L0_attn_bv, L0_attn_Wo, L0_attn_bo,
                 L0_ffn_gate_W_packed, L0_ffn_gate_scales, L0_ffn_up_W_packed,
                 L0_ffn_up_scales, L0_ffn_down_W_packed, L0_ffn_down_scales):
    x = np.ascontiguousarray(np.asarray(L0_attn_x, dtype=np.float32))

    def wt(w):
        return np.ascontiguousarray(np.asarray(w, np.float32).T)

    def pk(p):
        a = np.ascontiguousarray(np.asarray(p, np.int32)).view(np.uint8)[:, 0::4]
        a = np.ascontiguousarray(a).view(np.uint16)
        return np.ascontiguousarray(a.T)

    def sc(s):
        return np.ascontiguousarray(np.asarray(s, np.float32).T)

    shared = {
        "wqT": wt(L0_attn_Wq), "wkT": wt(L0_attn_Wk),
        "wvT": wt(L0_attn_Wv), "woT": wt(L0_attn_Wo),
        "bq": np.asarray(L0_attn_bq, np.float32).reshape(1, E),
        "bk": np.asarray(L0_attn_bk, np.float32).reshape(1, E),
        "bv": np.asarray(L0_attn_bv, np.float32).reshape(1, E),
        "bo": np.asarray(L0_attn_bo, np.float32).reshape(1, E),
        "gpT": pk(L0_ffn_gate_W_packed), "upT": pk(L0_ffn_up_W_packed),
        "dpT": pk(L0_ffn_down_W_packed),
        "sgT": sc(L0_ffn_gate_scales), "suT": sc(L0_ffn_up_scales),
        "sdT": sc(L0_ffn_down_scales),
        "onesb": np.ones((1, 512), np.float32).astype(BFNP),
    }
    inv_freq = 1.0 / 10000.0 ** (np.arange(0, D, 2, dtype=np.float32) / D)
    pos = np.arange(S, dtype=np.float32)
    freqs = np.outer(pos, inv_freq)
    emb = np.concatenate([freqs, freqs], axis=-1)
    cos = np.cos(emb).astype(np.float32)
    sin = np.sin(emb).astype(np.float32)
    sinp = sin.copy()
    sinp[:, :64] *= -1.0
    cos_fm = np.ascontiguousarray(cos.T)
    sinp_fm = np.ascontiguousarray(sinp.T)
    shared["cos_kv"] = cos_fm
    shared["sinp_kv"] = sinp_fm
    hs = np.zeros((128, 256), np.float32)
    for h in range(16):
        hs[:, 16 * h + h] = 1.0
    shared["hsel"] = hs.astype(BFNP)

    in_maps = []
    for c in range(8):
        b = c // 2
        q0 = TQ * (c % 2)
        mcore = dict(shared)
        mcore["x"] = np.ascontiguousarray(x[b])
        mcore["xq"] = np.ascontiguousarray(x[b, q0:q0 + TQ])
        mcore["cos_q"] = np.ascontiguousarray(cos_fm[:, q0:q0 + TQ])
        mcore["sinp_q"] = np.ascontiguousarray(sinp_fm[:, q0:q0 + TQ])
        in_maps.append(mcore)
    return in_maps


def make_nc(dbg=()):
    nc = bass.Bass("TRN2", target_bir_lowering=False)
    kb = build_core_program(nc, dbg=dbg)
    with nc.Block() as block:
        for eng in ENGS:
            def body(e, eng=eng):
                marks = list(kb.marks[eng]) + [(len(kb.ops[eng]), None)]
                mi = 0
                cur = None
                import contextlib
                ctx = contextlib.nullcontext()
                for i, fn in enumerate(kb.ops[eng]):
                    while mi < len(marks) - 1 and i >= marks[mi][0]:
                        if cur is not None:
                            ctx.__exit__(None, None, None)
                            cur = None
                        cur = marks[mi][1]
                        ctx = nc.named_scope(cur)
                        ctx.__enter__()
                        mi += 1
                    fn(e)
                if cur is not None:
                    ctx.__exit__(None, None, None)
            getattr(block, eng)(body)
    return nc


def kernel(**inputs):
    in_maps = host_prepare(**inputs)
    nc = make_nc()
    res = run_bass_kernel_spmd(nc, in_maps, core_ids=list(range(8)))
    outp = np.zeros((B, S, E), np.float32)
    for c in range(8):
        b, q0 = c // 2, TQ * (c % 2)
        outp[b, q0:q0 + TQ] = res.results[c]["out"]
    return outp


# version 05
# speedup vs baseline: 1.0705x; 1.0333x over previous
import math
from collections import defaultdict

import numpy as np
import ml_dtypes

import concourse.bass as bass
import concourse.mybir as mybir
from concourse.bass_utils import run_bass_kernel_spmd

F32 = mybir.dt.float32
BF16 = mybir.dt.bfloat16
U16 = mybir.dt.uint16
ALU = mybir.AluOpType
AFT = mybir.ActivationFunctionType
AX = mybir.AxisListType

B, S, E, H, D, M = 4, 4096, 2048, 16, 128, 16384
TQ = 2048
EC = E // 128
BFNP = np.dtype(ml_dtypes.bfloat16)
IDM = list(range(32))

ENGS = ("sync", "tensor", "vector", "scalar", "gpsimd")

PHASES = 5
NSUP = 16


class KB:

    def __init__(self, nc):
        self.nc = nc
        self.ops = {e: [] for e in ENGS}
        self.cnt = defaultdict(int)
        self.sems = {}
        self.marks = {e: [] for e in ENGS}

    def phase(self, name):
        for e in ENGS:
            self.marks[e].append((len(self.ops[e]), name))

    def sem(self, name):
        if name not in self.sems:
            self.sems[name] = self.nc.semaphore("sem_" + name).__enter__()
        return self.sems[name]

    def on(self, eng, fn):
        self.ops[eng].append(fn)

    def wait(self, eng, sem_name, val):
        if val <= 0:
            return
        s = self.sem(sem_name)
        self.on(eng, lambda e, s=s, v=val: e.wait_ge(s, v))

    def dma(self, eng, out, in_, **kw):
        name = eng + "d"
        s = self.sem(name)
        self.cnt[name] += 16
        v = self.cnt[name]
        self.on(eng, lambda e, o=out, i=in_, kw=dict(kw), s=s:
                e.dma_start(out=o, in_=i, **kw).then_inc(s, 16))
        return v

    def op(self, eng, meth, *args, **kw):
        self.on(eng, lambda e, m=meth, a=args, k=dict(kw): getattr(e, m)(*a, **k))

    def opi(self, eng, meth, *args, **kw):
        s = self.sem(eng)
        self.cnt[eng] += 1
        v = self.cnt[eng]
        self.on(eng, lambda e, m=meth, a=args, k=dict(kw), s=s:
                getattr(e, m)(*a, **k).then_inc(s, 1))
        return v

    def tiny(self, eng, meth, *args, **kw):
        v = self.opi(eng, meth, *args, **kw)
        self.wait(eng, eng, v)
        return v

    def barrier(self):
        snap = [(n, self.cnt[n]) for n in list(self.cnt) if self.cnt[n] > 0]
        for e in ENGS:
            for n, v in snap:
                if n == e:
                    continue
                self.wait(e, n, v)


def build_core_program(nc, dbg=()):
    kb = KB(nc)

    def dram(name, shape, dtype, kind="ExternalInput"):
        if kind == "Internal" and name in dbg:
            kind = "ExternalOutput"
        return nc.dram_tensor(name, list(shape), dtype, kind=kind).ap()

    x = dram("x", (S, E), F32)
    xq = dram("xq", (TQ, E), F32)
    wT = {n: dram(n, (E, E), F32) for n in ("wqT", "wkT", "wvT", "woT")}
    bvecs = {n: dram(n, (1, E), F32) for n in ("bq", "bk", "bv", "bo")}
    gpT = dram("gpT", (E // 4, M), U16)
    upT = dram("upT", (E // 4, M), U16)
    dpT = dram("dpT", (M // 4, E), U16)
    sgT = dram("sgT", (E // 32, M), F32)
    suT = dram("suT", (E // 32, M), F32)
    sdT = dram("sdT", (M // 32, E), F32)
    cos_kv = dram("cos_kv", (D, S), F32)
    sinp_kv = dram("sinp_kv", (D, S), F32)
    cos_q = dram("cos_q", (D, TQ), F32)
    sinp_q = dram("sinp_q", (D, TQ), F32)
    onesb = dram("onesb", (1, 512), BF16)
    hsel_d = dram("hsel", (128, 256), BF16)
    out = dram("out", (TQ, E), F32, kind="ExternalOutput")

    xntm = dram("xntm", (S, E), BF16, kind="Internal")
    xnqtm = dram("xnqtm", (TQ, E), BF16, kind="Internal")
    xnfm = dram("xnfm", (E, S), BF16, kind="Internal")
    xnqfm = dram("xnqfm", (E, TQ), BF16, kind="Internal")
    kcache = dram("kcache", (E, S), BF16, kind="Internal")
    vcache = dram("vcache", (S, H * 129), BF16, kind="Internal")
    qcache = dram("qcache", (E, TQ), BF16, kind="Internal")
    mtmp = dram("mtmp", (16, 1), F32, kind="Internal")
    mglob = dram("mglob", (1, 1), F32, kind="Internal")
    avsp = dram("avsp", (TQ, 2064), F32, kind="Internal")
    avn = dram("avn", (TQ, E), BF16, kind="Internal")
    x1sp = dram("x1sp", (TQ, E), F32, kind="Internal")
    xn2tm = dram("xn2tm", (TQ, E), BF16, kind="Internal")
    xn2fm = dram("xn2fm", (E, TQ), BF16, kind="Internal")
    acc = dram("acc", (TQ, E), F32, kind="Internal")
    sge = dram("sge", (E // 4, M), BF16, kind="Internal")
    sue = dram("sue", (E // 4, M), BF16, kind="Internal")
    sde = dram("sde", (M // 4, E), BF16, kind="Internal")
    wgdump = dram("wgdump", (128, 16384), BF16, kind="Internal")
    wddump = dram("wddump", (128, 16384), BF16, kind="Internal")
    itdump = dram("itdump", (128, 16384), BF16, kind="Internal")

    A = nc.sbuf_tensor("A", [128, 73728], BF16).__enter__()
    U = nc.sbuf_tensor("U", [128, 12288], U16).__enter__()
    Fa = nc.sbuf_tensor("Fa", [128, 4096], F32).__enter__()
    Tb = nc.sbuf_tensor("Tb", [128, 2048], F32).__enter__()
    ST = nc.sbuf_tensor("ST", [128, 32], F32).__enter__()
    epscol = nc.sbuf_tensor("epscol", [128, 1], F32).__enter__()
    lncol = nc.sbuf_tensor("lncol", [1, 1], F32).__enter__()
    negm = nc.sbuf_tensor("negm", [128, 1], F32).__enter__()
    onesr = nc.sbuf_tensor("onesr", [1, 512], BF16).__enter__()
    bias_sb = nc.sbuf_tensor("bias_sb", [1, 2048], BF16).__enter__()
    hsel = nc.sbuf_tensor("hsel_sb", [128, 256], BF16).__enter__()
    m16 = nc.sbuf_tensor("m16", [16, 8], F32).__enter__()
    m1 = nc.sbuf_tensor("m1", [1, 32], F32).__enter__()

    ps_a = nc.psum_tensor("ps_a", [128, 2048], F32).__enter__()
    ps_b = nc.psum_tensor("ps_b", [128, 1536], F32).__enter__()

    def a_(off, n):
        return A[:, off:off + n]

    kb.opi("vector", "memset", epscol[:], 1e-5)
    kb.opi("vector", "memset", lncol[:], -0.5 * math.log(float(D)))
    kb.opi("vector", "memset", m16[:, 0:2], 0.0)
    kb.dma("sync", onesr[:], onesb)
    kb.dma("sync", hsel[:], hsel_d)
    kb.barrier()

    kb.phase("p3_scaleprep")
    if PHASES >= 4:
        P126 = float(2.0 ** 126)
        uf32 = U[:].bitcast(F32)
        ub16 = U[:].bitcast(BF16)
        spv_mult = 0
        for srcT, dstE, nrow in ((sgT, sge, 64), (suT, sue, 64), (sdT, sde, 128)):
            nchunk = 4
            for ci in range(nchunk):
                if nrow == 64:
                    src_ap = srcT[:, ci * 4096:(ci + 1) * 4096]
                    ftile = uf32[0:64, 0:4096]
                    btile = ub16[0:64, 8192:12288]
                else:
                    src_ap = srcT[ci * 128:(ci + 1) * 128, :]
                    ftile = uf32[:, 0:2048]
                    btile = ub16[:, 8192:10240]
                kb.wait("sync", "vector", spv_mult)
                lv = kb.dma("sync", ftile, src_ap)
                kb.wait("vector", "syncd", lv)
                v_m2 = kb.opi("vector", "tensor_scalar", btile, ftile, P126,
                              None, ALU.mult)
                spv_mult = v_m2
                kb.wait("gpsimd", "vector", v_m2)
                stv = 0
                for e8 in range(8):
                    if nrow == 64:
                        dst_ap = dstE.rearrange("(a e) c -> a e c", e=8)[
                            :, e8, ci * 4096:(ci + 1) * 4096]
                    else:
                        dst_ap = dstE.rearrange("(a e) c -> a e c", e=8)[
                            ci * 128:(ci + 1) * 128, e8, :]
                    stv = kb.dma("gpsimd", dst_ap, btile)
                kb.wait("vector", "gpsimdd", stv)

    kb.phase("p0_ln1")
    def ln_pass(src, n_tiles, dst_tm):
        junk = a_(0, 2048)
        norm_done = {0: 0, 1: 0}
        store_done = {0: 0, 1: 0}
        v_sc = {}
        v_nmi = {}

        def stats(i):
            buf = i % 2
            xtile = Fa[:, buf * 2048:(buf + 1) * 2048]
            if i >= 2:
                kb.wait("sync", "scalar", norm_done[buf])
            xv = kb.dma("sync", xtile, src[i * 128:(i + 1) * 128, :])
            kb.wait("scalar", "syncd", xv)
            sb7 = 7 * buf
            s1, s2 = ST[:, sb7:sb7 + 1], ST[:, sb7 + 1:sb7 + 2]
            kb.op("scalar", "activation", junk, xtile, AFT.Identity, accum_out=s1)
            v_sc[i] = kb.opi("scalar", "activation", junk, xtile, AFT.Square,
                             accum_out=s2)

        def chain(i):
            buf = i % 2
            sb7 = 7 * buf
            s1, s2, mean, var, t0, inv, nmi = (ST[:, sb7 + j:sb7 + j + 1]
                                               for j in range(7))
            kb.wait("vector", "scalar", v_sc[i])
            kb.tiny("vector", "tensor_scalar", mean, s1, 1.0 / E, None, ALU.mult)
            kb.tiny("vector", "tensor_scalar", var, s2, 1.0 / E, None, ALU.mult)
            kb.tiny("vector", "tensor_tensor", t0, mean, mean, ALU.mult)
            v_ve = kb.tiny("vector", "tensor_sub", var, var, t0)
            return v_ve

        def norm(i, v_ve):
            buf = i % 2
            sb7 = 7 * buf
            s1, s2, mean, var, t0, inv, nmi = (ST[:, sb7 + j:sb7 + j + 1]
                                               for j in range(7))
            xtile = Fa[:, buf * 2048:(buf + 1) * 2048]
            t0s = ST[:, 14 + buf:15 + buf]
            kb.wait("scalar", "vector", v_ve)
            kb.tiny("scalar", "activation", t0s, var, AFT.Ln, bias=epscol[:])
            v_inv = kb.tiny("scalar", "activation", inv, t0s, AFT.Exp, scale=-0.5)
            kb.wait("vector", "scalar", v_inv)
            kb.tiny("vector", "tensor_scalar", nmi, mean, -1.0, None, ALU.mult)
            v_n = kb.tiny("vector", "tensor_tensor", nmi, nmi, inv, ALU.mult)
            v_nmi[i] = v_n
            kb.wait("scalar", "vector", v_n)
            xn_tm = a_(2048 + buf * 2048, 2048)
            if i >= 2:
                kb.wait("scalar", "scalard", store_done[buf])
            norm_done[buf] = kb.opi("scalar", "activation", xn_tm, xtile,
                                    AFT.Identity, bias=nmi, scale=inv)
            kb.wait("scalar", "scalar", norm_done[buf])
            store_done[buf] = kb.dma(
                "scalar", dst_tm[i * 128:(i + 1) * 128, :], xn_tm)

        stats(0)
        ves = {0: chain(0)}
        for i in range(1, n_tiles):
            stats(i)
            ves[i] = chain(i)
            norm(i - 1, ves[i - 1])
        norm(n_tiles - 1, ves[n_tiles - 1])

    def fm_transpose(src_tm, dst_fm, ncols):
        st_done = {0: 0, 1: 0}
        for ec in range(EC):
            buf = ec % 2
            tile = a_(buf * 8192, ncols)
            if ec >= 2:
                kb.wait("sync", "syncd", st_done[buf])
            v_tr = kb.dma("sync", tile, src_tm[:, ec * 128:(ec + 1) * 128],
                          transpose=True)
            kb.wait("sync", "syncd", v_tr)
            st_done[buf] = kb.dma("sync", dst_fm[ec * 128:(ec + 1) * 128, :], tile)

    if PHASES >= 0:
        ln_pass(x, S // 128, xntm)
        kb.barrier()
        ln_pass(xq, TQ // 128, xnqtm)
        kb.barrier()
        fm_transpose(xntm, xnfm, S)
        fm_transpose(xnqtm, xnqfm, TQ)
        kb.barrier()

    def proj_kq(wname, bname, kind):
        WT_OFF = 16384
        wv = 0
        for ec in range(EC):
            wv = kb.dma("gpsimd", a_(WT_OFF + ec * 2048, 2048),
                        wT[wname][ec * 128:(ec + 1) * 128, :])
        bv_ = kb.dma("gpsimd", bias_sb[:], bvecs[bname])
        kb.wait("tensor", "gpsimdd", wv)
        kb.wait("tensor", "gpsimdd", bv_)
        src_fm = xnqfm if kind == "q" else xnfm
        n_tc = (TQ if kind == "q" else S) // 512
        cos_d = cos_q if kind == "q" else cos_kv
        sin_d = sinp_q if kind == "q" else sinp_kv
        dstc = qcache if kind == "q" else kcache
        xn_used = {0: 0, 1: 0}
        tab_used = {0: 0, 1: 0}
        pk_free = [0, 0, 0, 0]
        kt_free = [0, 0, 0, 0]
        sq_free = [0, 0]
        nrm_free = 0
        col = 0 if kind == "k" else 1
        for tc in range(n_tc):
            buf = tc % 2
            if tc >= 2:
                kb.wait("sync", "tensor", xn_used[buf])
                kb.wait("sync", "vector", tab_used[buf])
            xv = 0
            for ec in range(EC):
                xv = kb.dma("sync", a_(buf * 8192 + ec * 512, 512),
                            src_fm[ec * 128:(ec + 1) * 128, tc * 512:(tc + 1) * 512])
            kb.dma("sync", Tb[:, buf * 1024:buf * 1024 + 512],
                   cos_d[:, tc * 512:(tc + 1) * 512])
            tv = kb.dma("sync", Tb[:, buf * 1024 + 512:buf * 1024 + 1024],
                        sin_d[:, tc * 512:(tc + 1) * 512])
            kb.wait("tensor", "syncd", xv)
            kb.wait("vector", "syncd", tv)
            for ds in range(EC):
                pbuf = (tc * EC + ds) % 4
                pk = ps_a[:, pbuf * 512:(pbuf + 1) * 512]
                kb.wait("tensor", "vector", pk_free[pbuf])
                for ec in range(EC):
                    kb.op("tensor", "matmul", pk,
                          a_(WT_OFF + ec * 2048 + ds * 128, 128),
                          a_(buf * 8192 + ec * 512, 512),
                          start=(ec == 0), stop=False)
                mv = kb.opi("tensor", "matmul", pk,
                            bias_sb[:, ds * 128:ds * 128 + 128], onesr[:],
                            start=False, stop=True)
                kb.wait("vector", "tensor", mv)
                qsht = Fa[:, (pbuf % 2) * 512:(pbuf % 2) * 512 + 512]
                kb.op("vector", "stream_shuffle", qsht[0:64, :], pk[64:128, :], IDM)
                kb.op("vector", "stream_shuffle", qsht[64:128, :], pk[0:64, :], IDM)
                t1 = a_(49152 + (pbuf % 2) * 512, 512)
                kb.op("vector", "tensor_tensor", t1, pk,
                      Tb[:, buf * 1024:buf * 1024 + 512], ALU.mult)
                t2 = a_(50176 + (pbuf % 2) * 512, 512)
                kb.op("vector", "tensor_tensor", t2, qsht,
                      Tb[:, buf * 1024 + 512:buf * 1024 + 1024], ALU.mult)
                ktb = (tc * EC + ds) % 4
                ktile = a_(51200 + ktb * 512, 512)
                if kt_free[ktb]:
                    kb.wait("vector", "scalard", kt_free[ktb])
                v_kt = kb.opi("vector", "tensor_tensor", ktile, t1, t2, ALU.add)
                pk_free[pbuf] = v_kt
                kb.wait("scalar", "vector", v_kt)
                kt_free[ktb] = kb.dma(
                    "scalar", dstc[ds * 128:(ds + 1) * 128, tc * 512:(tc + 1) * 512],
                    ktile)
                sqb = (tc * EC + ds) % 2
                sq = a_(53248 + sqb * 512, 512)
                kb.wait("scalar", "tensor", sq_free[sqb])
                v_sq = kb.opi("scalar", "activation", sq, ktile, AFT.Square)
                kb.wait("tensor", "scalar", v_sq)
                if ds == 0:
                    kb.wait("tensor", "vector", nrm_free)
                v_mm = kb.opi("tensor", "matmul", ps_b[0:16, 1024:1536],
                              hsel[:, ds * 16:ds * 16 + 16], sq,
                              start=(ds == 0), stop=(ds == EC - 1),
                              skip_group_check=True)
                sq_free[sqb] = v_mm
            xn_used[buf] = v_mm
            tab_used[buf] = kb.cnt["vector"]
            kb.wait("vector", "tensor", v_mm)
            kb.tiny("vector", "reduce_max", m16[:, 2:3], ps_b[0:16, 1024:1536], AX.X)
            nrm_free = kb.tiny("vector", "tensor_max", m16[:, col:col + 1],
                               m16[:, col:col + 1], m16[:, 2:3])

    def proj_v():
        WT_OFF = 16384
        wv = 0
        for ec in range(EC):
            wv = kb.dma("gpsimd", a_(WT_OFF + ec * 2048, 2048),
                        wT["wvT"][ec * 128:(ec + 1) * 128, :])
        bv_ = kb.dma("gpsimd", bias_sb[:], bvecs["bv"])
        kb.wait("tensor", "gpsimdd", wv)
        kb.wait("tensor", "gpsimdd", bv_)
        xn_used = {0: 0, 1: 0}
        pv_free = {0: 0, 1: 0}
        vt_free = {0: 0, 1: 0}
        for tc in range(S // 512):
            buf = tc % 2
            if tc >= 2:
                kb.wait("sync", "tensor", xn_used[buf])
            xv = 0
            for ec in range(EC):
                xv = kb.dma("sync", a_(buf * 8192 + ec * 512, 512),
                            xnfm[ec * 128:(ec + 1) * 128, tc * 512:(tc + 1) * 512])
            kb.wait("tensor", "syncd", xv)
            for st_i in range(4):
                vtb = (tc * 4 + st_i) % 2
                vtile = a_(49152 + vtb * 2064, 2064)
                vt_r = vtile.rearrange("p (h c) -> p h c", h=16)
                if vt_free[vtb]:
                    kb.wait("vector", "scalard", vt_free[vtb])
                v_ms = kb.opi("vector", "memset", vt_r[:, :, 128:129], 1.0)
                for dc in range(4):
                    pbuf = (tc * 16 + st_i * 4 + dc) % 2
                    pv = ps_b[:, pbuf * 512:(pbuf + 1) * 512]
                    kb.wait("tensor", "scalar", pv_free[pbuf])
                    for ec in range(EC):
                        kb.op("tensor", "matmul", pv,
                              a_(buf * 8192 + ec * 512 + st_i * 128, 128),
                              a_(WT_OFF + ec * 2048 + dc * 512, 512),
                              start=(ec == 0), stop=False)
                    mv = kb.opi("tensor", "matmul", pv, onesr[:, 0:128],
                                bias_sb[:, dc * 512:(dc + 1) * 512],
                                start=False, stop=True)
                    kb.wait("scalar", "tensor", mv)
                    if dc == 0:
                        kb.wait("scalar", "vector", v_ms)
                    pv_free[pbuf] = kb.opi(
                        "scalar", "activation",
                        vt_r[:, dc * 4:dc * 4 + 4, 0:128],
                        pv.rearrange("p (a c) -> p a c", a=4), AFT.Identity)
                kb.wait("scalar", "scalar", kb.cnt["scalar"])
                vt_free[vtb] = kb.dma(
                    "scalar",
                    vcache[tc * 512 + st_i * 128: tc * 512 + (st_i + 1) * 128, :],
                    vtile)
            xn_used[buf] = kb.cnt["tensor"]

    if PHASES >= 1:
        kb.phase("p1k")
        proj_kq("wkT", "bk", "k")
        kb.barrier()
        kb.phase("p1v")
        proj_v()
        kb.barrier()
        kb.phase("p1q")
        proj_kq("wqT", "bq", "q")
        kb.barrier()
        kb.tiny("vector", "tensor_tensor", m16[:, 3:4], m16[:, 0:1], m16[:, 1:2],
                ALU.mult)
        kb.wait("gpsimd", "vector", kb.cnt["vector"])
        mv1 = kb.dma("gpsimd", mtmp, m16[:, 3:4])
        kb.wait("sync", "gpsimdd", mv1)
        mv2 = kb.dma("sync", m1[:, 0:16], mtmp.rearrange("a b -> b a"))
        kb.wait("vector", "syncd", mv2)
        v_mx = kb.tiny("vector", "reduce_max", m1[:, 16:17], m1[:, 0:16], AX.X)
        kb.wait("scalar", "vector", v_mx)
        kb.tiny("scalar", "activation", m1[:, 17:18], m1[:, 16:17], AFT.Ln)
        v_m = kb.tiny("scalar", "activation", m1[:, 18:19], m1[:, 17:18], AFT.Exp,
                      scale=0.5, bias=lncol[:])
        kb.wait("vector", "scalar", v_m)
        v_neg = kb.opi("vector", "tensor_scalar", m1[:, 19:20], m1[:, 18:19],
                       -1.0, None, ALU.mult)
        kb.wait("gpsimd", "vector", v_neg)
        mv3 = kb.dma("gpsimd", mglob, m1[:, 19:20])
        kb.wait("sync", "gpsimdd", mv3)
        kb.dma("sync", negm[:], mglob.broadcast_to([128, 1]))
        kb.barrier()

    SM = nc.sbuf_tensor("SM", [128, 48], F32).__enter__()
    if PHASES >= 2:
        kb.phase("p2_attn")
        for half in (0, 1):
            kv = kb.dma("sync", A[:, 0:32768].rearrange("d (h s) -> d h s", h=16),
                        kcache.rearrange("(h d) s -> d h s", d=128)
                        [:, :, half * 2048:(half + 1) * 2048])
            kv2 = kb.dma("sync", A[:, 32768:65792].rearrange("p (a c) -> p a c", a=16),
                         vcache.rearrange("(a p) c -> p a c", p=128)
                         [:, half * 16:(half + 1) * 16, :])
            kb.wait("tensor", "syncd", kv)
            kb.wait("tensor", "syncd", kv2)
            qsb_used = {0: 0, 1: 0}
            sc_free = {0: 0, 1: 0}
            p_free = {0: 0, 1: 0}
            av_free_s = 0
            av_free_v = 0
            sp_used = 0
            avn_used = 0
            spv = 0
            for qb in range(16):
                qbuf = qb % 2
                if qb >= 2:
                    kb.wait("sync", "tensor", qsb_used[qbuf])
                qv = kb.dma("sync",
                            A[:, 65792 + qbuf * 2048:65792 + (qbuf + 1) * 2048]
                            .rearrange("d (h t) -> d h t", h=16),
                            qcache.rearrange("(h d) t -> d h t", d=128)
                            [:, :, qb * 128:(qb + 1) * 128])
                kb.wait("tensor", "syncd", qv)
                if half == 1:
                    kb.wait("sync", "vector", sp_used)
                    spv = kb.dma("sync", Fa[:, 0:2064],
                                 avsp[qb * 128:(qb + 1) * 128, :])
                kb.wait("tensor", "scalar", av_free_s)
                kb.wait("tensor", "vector", av_free_v)
                v_last = 0
                iters = [(h, grp) for h in range(16) for grp in range(4)]

                def emit_scores(k):
                    h, grp = iters[k]
                    pb = k % 2
                    sc = ps_b[:, 512 * (1 + pb):512 * (2 + pb)]
                    kb.wait("tensor", "scalar", sc_free[pb])
                    for ss in range(4):
                        emit = kb.opi if ss == 3 else kb.op
                        r = emit("tensor", "matmul",
                                 sc[:, ss * 128:(ss + 1) * 128],
                                 a_(h * 2048 + grp * 512 + ss * 128, 128),
                                 a_(65792 + qbuf * 2048 + h * 128, 128),
                                 start=True, stop=True)
                    return r

                def emit_exp(k, v_scmm):
                    pb = k % 2
                    sc = ps_b[:, 512 * (1 + pb):512 * (2 + pb)]
                    pgrp = a_(69888 + pb * 512, 512)
                    kb.wait("scalar", "tensor", max(v_scmm, p_free[pb]))
                    sc_free[pb] = kb.opi("scalar", "activation", pgrp, sc,
                                         AFT.Exp,
                                         scale=float(1.0 / math.sqrt(D)),
                                         bias=negm[:])
                    return sc_free[pb]

                def emit_av(k):
                    h, grp = iters[k]
                    pb = k % 2
                    pgrp = a_(69888 + pb * 512, 512)
                    kb.wait("tensor", "scalar", sc_free[pb])
                    for ss in range(4):
                        st_idx = grp * 4 + ss
                        vb = 32768 + st_idx * 2064 + 129 * h
                        kb.op("tensor", "matmul",
                              ps_a[:, h * 128:(h + 1) * 128],
                              pgrp[:, ss * 128:(ss + 1) * 128],
                              A[:, vb:vb + 128],
                              start=(ss == 0 and grp == 0 and (h % 4) == 0),
                              stop=False, skip_group_check=True)
                        p_free[pb] = kb.opi(
                            "tensor", "matmul", ps_b[:, h:h + 1],
                            pgrp[:, ss * 128:(ss + 1) * 128],
                            A[:, vb + 128:vb + 129],
                            start=(ss == 0 and grp == 0 and h == 0),
                            stop=False, skip_group_check=True)
                    return p_free[pb]

                v_sc_prev = emit_scores(0)
                emit_exp(0, v_sc_prev)
                for k in range(1, 64):
                    v_sc_k = emit_scores(k)
                    emit_exp(k, v_sc_k)
                    v_last = emit_av(k - 1)
                v_last = emit_av(63)
                if half == 0:
                    kb.wait("scalar", "tensor", v_last)
                    kb.op("scalar", "activation", Fa[:, 0:2048], ps_a[:], AFT.Identity)
                    av_free_s = kb.opi("scalar", "activation", Fa[:, 2048:2064],
                                       ps_b[:, 0:16], AFT.Identity)
                    kb.wait("scalar", "scalar", av_free_s)
                    kb.dma("scalar", avsp[qb * 128:(qb + 1) * 128, :], Fa[:, 0:2064])
                else:
                    kb.wait("vector", "tensor", v_last)
                    kb.wait("vector", "syncd", spv)
                    kb.tiny("vector", "tensor_add", SM[:, 0:16], ps_b[:, 0:16],
                            Fa[:, 2048:2064])
                    kb.tiny("vector", "reciprocal", SM[:, 16:32], SM[:, 0:16])
                    av_free_v = kb.opi("vector", "tensor_add", Fa[:, 0:2048],
                                       ps_a[:], Fa[:, 0:2048])
                    avn_sb = a_(70912, 2048)
                    if avn_used:
                        kb.wait("vector", "gpsimdd", avn_used)
                    v_mul = 0
                    for h in range(16):
                        emit = kb.opi if h == 15 else kb.op
                        r = emit("vector", "tensor_tensor",
                                 avn_sb[:, h * 128:(h + 1) * 128],
                                 Fa[:, h * 128:(h + 1) * 128],
                                 SM[:, 16 + h:17 + h].broadcast_to([128, 128]),
                                 ALU.mult)
                        if h == 15:
                            v_mul = r
                    sp_used = v_mul
                    kb.wait("gpsimd", "vector", v_mul)
                    avn_used = kb.dma("gpsimd", avn[qb * 128:(qb + 1) * 128, :],
                                      avn_sb)
                qsb_used[qbuf] = kb.cnt["tensor"]
            kb.barrier()

    if PHASES >= 3:
        kb.phase("p25_oproj")
        wv = 0
        for ec in range(EC):
            wv = kb.dma("gpsimd", a_(ec * 2048, 2048),
                        wT["woT"][ec * 128:(ec + 1) * 128, :])
        bv_ = kb.dma("gpsimd", bias_sb[:], bvecs["bo"])
        kb.wait("tensor", "gpsimdd", wv)
        kb.wait("tensor", "gpsimdd", bv_)
        junk2 = a_(36864, 512)
        xn2_used_c = 0
        v_x1_prev = 0
        avt_used = 0
        x1_used = 0
        xn2_used = 0
        po_free = {0: 0, 1: 0}
        avv_q = {0: 0, 1: 0}

        def avt_load(qb):
            ab = qb % 2
            kb.wait("sync", "tensor", avt_used)
            vv = 0
            for ec in range(EC):
                vv = kb.dma("sync", a_(32768 + ab * 2048 + ec * 128, 128),
                            avn[qb * 128:(qb + 1) * 128, ec * 128:(ec + 1) * 128],
                            transpose=True)
            avv_q[ab] = vv

        avt_load(0)
        for qb in range(16):
            ab = qb % 2
            if qb < 15:
                avt_load(qb + 1)
            kb.wait("sync", "scalar", xn2_used_c)
            xqv = kb.dma("sync", Fa[:, 2048:4096], xq[qb * 128:(qb + 1) * 128, :])
            kb.wait("tensor", "syncd", avv_q[ab])
            kb.wait("scalar", "vector", v_x1_prev)
            for eoc in range(4):
                pb = eoc % 2
                po = ps_b[:, pb * 512:(pb + 1) * 512]
                kb.wait("tensor", "scalar", po_free[pb])
                for ec in range(EC):
                    kb.op("tensor", "matmul", po,
                          a_(32768 + ab * 2048 + ec * 128, 128),
                          a_(ec * 2048 + eoc * 512, 512),
                          start=(ec == 0), stop=False)
                mv = kb.opi("tensor", "matmul", po, onesr[:, 0:128],
                            bias_sb[:, eoc * 512:(eoc + 1) * 512],
                            start=False, stop=True)
                kb.wait("scalar", "tensor", mv)
                kb.op("scalar", "activation", Fa[:, eoc * 512:(eoc + 1) * 512],
                      po, AFT.Identity, accum_out=ST[:, 8 + eoc:9 + eoc])
                po_free[pb] = kb.opi("scalar", "activation", junk2, po,
                                     AFT.Square, accum_out=ST[:, 12 + eoc:13 + eoc])
            avt_used = mv
            s1, s2, mean, var, t0, inv, nmi = (ST[:, j:j + 1] for j in range(7))
            kb.wait("vector", "scalar", po_free[pb])
            kb.tiny("vector", "reduce_sum", s1, ST[:, 8:12], AX.X)
            kb.tiny("vector", "reduce_sum", s2, ST[:, 12:16], AX.X)
            kb.tiny("vector", "tensor_scalar", mean, s1, 1.0 / E, None, ALU.mult)
            kb.tiny("vector", "tensor_scalar", var, s2, 1.0 / E, None, ALU.mult)
            kb.tiny("vector", "tensor_tensor", t0, mean, mean, ALU.mult)
            v_ve = kb.tiny("vector", "tensor_sub", var, var, t0)
            kb.wait("scalar", "vector", v_ve)
            kb.tiny("scalar", "activation", t0, var, AFT.Ln, bias=epscol[:])
            v_inv = kb.tiny("scalar", "activation", inv, t0, AFT.Exp, scale=-0.5)
            kb.wait("vector", "scalar", v_inv)
            kb.tiny("vector", "tensor_scalar", nmi, mean, -1.0, None, ALU.mult)
            v_nmi = kb.tiny("vector", "tensor_tensor", nmi, nmi, inv, ALU.mult)
            kb.wait("scalar", "vector", v_nmi)
            v_n2 = kb.opi("scalar", "activation", Fa[:, 0:2048], Fa[:, 0:2048],
                          AFT.Identity, bias=nmi, scale=inv)
            kb.wait("vector", "scalar", v_n2)
            kb.wait("vector", "syncd", xqv)
            if x1_used:
                kb.wait("vector", "gpsimdd", x1_used)
            v_x1 = kb.opi("vector", "tensor_add", Fa[:, 2048:4096],
                          Fa[:, 2048:4096], Fa[:, 0:2048])
            kb.wait("gpsimd", "vector", v_x1)
            x1_used = kb.dma("gpsimd", x1sp[qb * 128:(qb + 1) * 128, :],
                             Fa[:, 2048:4096])
            kb.wait("scalar", "vector", v_x1)
            for c4 in range(4):
                kb.op("scalar", "activation", junk2,
                      Fa[:, 2048 + c4 * 512:2048 + (c4 + 1) * 512],
                      AFT.Identity, accum_out=ST[:, 8 + c4:9 + c4])
            for c4 in range(4):
                r = (kb.opi if c4 == 3 else kb.op)(
                    "scalar", "activation", junk2,
                    Fa[:, 2048 + c4 * 512:2048 + (c4 + 1) * 512],
                    AFT.Square, accum_out=ST[:, 12 + c4:13 + c4])
            v_st3 = r
            kb.wait("vector", "scalar", v_st3)
            kb.tiny("vector", "reduce_sum", s1, ST[:, 8:12], AX.X)
            kb.tiny("vector", "reduce_sum", s2, ST[:, 12:16], AX.X)
            kb.tiny("vector", "tensor_scalar", mean, s1, 1.0 / E, None, ALU.mult)
            kb.tiny("vector", "tensor_scalar", var, s2, 1.0 / E, None, ALU.mult)
            kb.tiny("vector", "tensor_tensor", t0, mean, mean, ALU.mult)
            v_ve = kb.tiny("vector", "tensor_sub", var, var, t0)
            kb.wait("scalar", "vector", v_ve)
            kb.tiny("scalar", "activation", t0, var, AFT.Ln, bias=epscol[:])
            v_inv = kb.tiny("scalar", "activation", inv, t0, AFT.Exp, scale=-0.5)
            kb.wait("vector", "scalar", v_inv)
            kb.tiny("vector", "tensor_scalar", nmi, mean, -1.0, None, ALU.mult)
            v_nmi = kb.tiny("vector", "tensor_tensor", nmi, nmi, inv, ALU.mult)
            kb.wait("scalar", "vector", v_nmi)
            xn2sb = a_(37376, 2048)
            if xn2_used:
                kb.wait("scalar", "scalard", xn2_used)
            v_xn2 = kb.opi("scalar", "activation", xn2sb, Fa[:, 2048:4096],
                           AFT.Identity, bias=nmi, scale=inv)
            xn2_used_c = v_xn2
            v_x1_prev = v_x1
            kb.wait("scalar", "scalar", v_xn2)
            xn2_used = kb.dma("scalar", xn2tm[qb * 128:(qb + 1) * 128, :], xn2sb)
        kb.barrier()

    def nib_ops(r):
        masks = [(0x0007, 6, ALU.logical_shift_left, 0x0008, 12),
                 (0x0070, 2, ALU.logical_shift_left, 0x0080, 8),
                 (0x0700, 2, ALU.logical_shift_right, 0x0800, 4),
                 (0x7000, 6, ALU.logical_shift_right, 0x8000, 0)]
        return masks[r]

    if PHASES >= 4:
        kb.barrier()
        fm_transpose(xn2tm, xn2fm, TQ)
        kb.barrier()
        for bb in range(EC):
            rstart = 512 * (bb % 4) + bb // 4
            v_t = kb.dma("sync", a_(bb * 2048, 2048),
                         xn2fm[rstart:rstart + 509:4, :])
        kb.wait("tensor", "syncd", v_t)
        kb.barrier()

        kb.phase("p3_ffn")
        R1 = 32768
        ITR = 49152
        SCG, SCU, SCD = 65536, 67584, 69632
        U32 = mybir.dt.uint32
        EM0, EM1, SGT_, GGT = 71680, 72192, 72704, 73216
        em2 = [a_(EM0, 512), a_(EM1, 512)]
        sg32 = a_(SGT_, 512).bitcast(U32)
        gg = a_(GGT, 512)

        def nib32(r):
            msk, sh, shop, msk2, sh2 = nib_ops(r)
            return (msk | (msk << 16), sh, shop, msk2 | (msk2 << 16), sh2)

        vv_w = {"vector": 0, "gpsimd": 0}
        em_free2 = [(0, 0), (0, 0)]

        def dequant(usrc_u16, wslot, scales_ap):
            nonlocal nidx
            usrc = usrc_u16.bitcast(U32)
            for r in range(4):
                msk, sh, shop, msk2, sh2 = nib32(r)
                eb = em2[nidx % 2]
                eb32 = eb.bitcast(U32)
                kb.wait("vector", "vector", em_free2[nidx % 2][0])
                kb.wait("vector", "gpsimd", em_free2[nidx % 2][1])
                kb.op("vector", "tensor_scalar", eb32, usrc,
                      msk, sh, ALU.bitwise_and, shop)
                if sh2:
                    kb.op("vector", "tensor_scalar", sg32, usrc,
                          msk2, sh2, ALU.bitwise_and, ALU.logical_shift_left)
                else:
                    kb.op("vector", "tensor_scalar", sg32, usrc,
                          msk2, None, ALU.bitwise_and)
                v_or = kb.opi("vector", "tensor_tensor", eb32, eb32,
                              sg32, ALU.bitwise_or)
                eng = "gpsimd" if nidx % 2 == 0 else "vector"
                if eng == "gpsimd":
                    kb.wait("gpsimd", "vector", v_or)
                vv_w[eng] = kb.opi(eng, "tensor_tensor", a_(wslot(r), 512),
                                   eb.bitcast(BF16), scales_ap, ALU.mult)
                em_free2[nidx % 2] = ((vv_w["vector"], kb.cnt["gpsimd"])
                                      if eng == "vector" else
                                      (kb.cnt["vector"], vv_w["gpsimd"]))
                nidx += 1
        acc_dma = [0] * 16
        em_free = {0: 0, 1: 0}
        u_free = {0: 0, 1: 0}
        ud_free = {0: 0, 1: 0}
        nidx = 0
        v_gu_last = 0
        gg_free = 0
        dq_free = 0
        itr_free = 0
        dbuf_free = {0: 0, 1: 0}
        psd_free = {0: 0, 1: 0}
        psg_free = {0: 0, 1: 0}
        psu_free = {0: 0, 1: 0}
        scl_free = 0
        scl_free_v = 0
        for sup in range(NSUP):
            for cpar in range(2):
                j = sup * 2 + cpar
                ub = j % 2
                kb.wait("sync", "vector", u_free[ub])
                pv = 0
                for a4 in range(4):
                    pv = kb.dma("sync", U[:, ub * 2048 + a4 * 512:ub * 2048 + (a4 + 1) * 512],
                                gpT[a4 * 128:(a4 + 1) * 128, j * 512:(j + 1) * 512])
                    pv = kb.dma("sync", U[:, 4096 + ub * 2048 + a4 * 512:4096 + ub * 2048 + (a4 + 1) * 512],
                                upT[a4 * 128:(a4 + 1) * 128, j * 512:(j + 1) * 512])
                kb.wait("sync", "gpsimd", scl_free)
                kb.wait("sync", "vector", scl_free_v)
                sv = 0
                for a4 in range(4):
                    sv = kb.dma("sync", a_(SCG + a4 * 512, 512),
                                sge[a4 * 128:(a4 + 1) * 128, j * 512:(j + 1) * 512])
                    sv = kb.dma("sync", a_(SCU + a4 * 512, 512),
                                sue[a4 * 128:(a4 + 1) * 128, j * 512:(j + 1) * 512])
                kb.wait("vector", "syncd", pv)
                kb.wait("vector", "syncd", sv)
                kb.wait("gpsimd", "syncd", pv)
                kb.wait("gpsimd", "syncd", sv)
                if cpar == 0:
                    kb.wait("vector", "tensor", dq_free)
                    kb.wait("gpsimd", "tensor", dq_free)
                for mi, (uoff, woff, scoff) in enumerate(
                        ((0, R1, SCG), (4096, R1 + 8192, SCU))):
                    for a4 in range(4):
                        usrc = U[:, uoff + ub * 2048 + a4 * 512:uoff + ub * 2048 + (a4 + 1) * 512]
                        dequant(usrc,
                                (lambda r, a4=a4, woff=woff: woff + (4 * r + a4) * 512),
                                a_(scoff + a4 * 512, 512))
                v_w_v, v_w_g = vv_w["vector"], vv_w["gpsimd"]
                scl_free = v_w_g
                scl_free_v = v_w_v
                u_free[ub] = kb.cnt["vector"]
                if j == 0 and mi == 1 and "wgdump" in dbg:
                    kb.wait("gpsimd", "gpsimd", v_w)
                    kb.dma("gpsimd", wgdump, a_(R1, 16384))

                kb.wait("tensor", "gpsimd", v_w_g)
                kb.wait("tensor", "vector", v_w_v)
                for rn in range(4):
                    for tch in range(4):
                        gpb = tch % 2
                        psg = ps_a[:, gpb * 512:(gpb + 1) * 512]
                        psu = ps_a[:, 1024 + gpb * 512:1024 + (gpb + 1) * 512]
                        kb.wait("tensor", "scalar", psg_free[gpb])
                        kb.wait("tensor", "vector", psu_free[gpb])
                        for bb in range(EC):
                            kb.op("tensor", "matmul", psg,
                                  A[:, R1 + bb * 512 + rn:R1 + (bb + 1) * 512:4],
                                  a_(bb * 2048 + tch * 512, 512),
                                  start=(bb == 0), stop=(bb == EC - 1))
                        v_gu = 0
                        for bb in range(EC):
                            emit = kb.opi if bb == EC - 1 else kb.op
                            r2 = emit("tensor", "matmul", psu,
                                      A[:, R1 + 8192 + bb * 512 + rn:R1 + 8192 + (bb + 1) * 512:4],
                                      a_(bb * 2048 + tch * 512, 512),
                                      start=(bb == 0), stop=(bb == EC - 1))
                            if bb == EC - 1:
                                v_gu = r2
                        v_gu_last = v_gu
                        kb.wait("scalar", "tensor", v_gu)
                        kb.wait("scalar", "vector", gg_free)
                        psg_free[gpb] = kb.opi("scalar", "activation", gg, psg,
                                               AFT.Gelu_apprx_tanh)
                        kb.wait("vector", "scalar", psg_free[gpb])
                        psu_free[gpb] = kb.opi(
                            "vector", "tensor_tensor",
                            a_(ITR + cpar * 8192 + rn * 2048 + tch * 512, 512),
                            psu, gg, ALU.mult)
                        gg_free = psu_free[gpb]
            kb.wait("vector", "tensor", v_gu_last)
            kb.wait("gpsimd", "tensor", v_gu_last)
            for cpar in range(2):
                j = sup * 2 + cpar
                ub = j % 2
                kb.wait("sync", "vector", ud_free[ub])
                kb.wait("sync", "gpsimd", kb.cnt["gpsimd"])
                dv = kb.dma("sync", U[:, 8192 + ub * 2048:8192 + (ub + 1) * 2048],
                            dpT[j * 128:(j + 1) * 128, :])
                sv = kb.dma("sync", a_(SCD, 2048), sde[j * 128:(j + 1) * 128, :])
                kb.wait("vector", "syncd", dv)
                kb.wait("vector", "syncd", sv)
                kb.wait("gpsimd", "syncd", dv)
                kb.wait("gpsimd", "syncd", sv)
                for c4 in range(4):
                    usrc = U[:, 8192 + ub * 2048 + c4 * 512:8192 + ub * 2048 + (c4 + 1) * 512]
                    dequant(usrc,
                            (lambda r, c4=c4, cpar=cpar: R1 + cpar * 8192 + r * 2048 + c4 * 512),
                            a_(SCD + c4 * 512, 512))
                ud_free[ub] = kb.cnt["vector"]
            v_wd_v, v_wd_g = vv_w["vector"], vv_w["gpsimd"]
            kb.wait("tensor", "gpsimd", v_wd_g)
            kb.wait("tensor", "vector", v_wd_v)
            if sup == 0 and "wddump" in dbg:
                kb.wait("gpsimd", "gpsimd", v_wd_g)
                kb.wait("gpsimd", "vector", v_wd_v)
                kb.dma("gpsimd", wddump, a_(R1, 16384))
                kb.wait("gpsimd", "vector", kb.cnt["vector"])
                kb.dma("gpsimd", itdump, a_(ITR, 16384))
            for tsub in range(16):
                dbuf = tsub % 2
                kb.wait("scalar", "gpsimdd", dbuf_free[dbuf])
                for hc in range(4):
                    pdb = hc % 2
                    psd = ps_b[:, pdb * 512:(pdb + 1) * 512]
                    kb.wait("tensor", "scalar", psd_free[pdb])
                    nmm = 0
                    for cpar in range(2):
                        for r in range(4):
                            emit = kb.opi if nmm == 7 else kb.op
                            r3 = emit("tensor", "matmul", psd,
                                      a_(ITR + cpar * 8192 + r * 2048 + tsub * 128, 128),
                                      a_(R1 + cpar * 8192 + r * 2048 + hc * 512, 512),
                                      start=(nmm == 0), stop=(nmm == 7))
                            if nmm == 7:
                                v_dn = r3
                            nmm += 1
                    kb.wait("scalar", "tensor", v_dn)
                    psd_free[pdb] = kb.opi(
                        "scalar", "activation",
                        Fa[:, dbuf * 2048 + hc * 512:dbuf * 2048 + (hc + 1) * 512],
                        psd, AFT.Identity)
                kb.wait("gpsimd", "scalar", psd_free[pdb])
                if acc_dma[tsub]:
                    kb.wait("gpsimd", "gpsimdd", acc_dma[tsub])
                kw = {} if sup == 0 else dict(accum_op=ALU.add)
                acc_dma[tsub] = kb.dma(
                    "gpsimd", acc[tsub * 128:(tsub + 1) * 128, :],
                    Fa[:, dbuf * 2048:(dbuf + 1) * 2048],
                    max_dma_last_dim=8192, **kw)
                dbuf_free[dbuf] = acc_dma[tsub]
            dq_free = v_dn
            itr_free = v_dn
        kb.barrier()

    if PHASES >= 5:
        kb.phase("p4_final")
        junk4 = a_(16384, 2048)
        out_done = {0: 0, 1: 0}
        norm4 = {0: 0, 1: 0}
        addv4 = {0: 0, 1: 0}
        for tsub in range(16):
            buf = tsub % 2
            acct = Fa[:, buf * 2048:(buf + 1) * 2048]
            x1t = a_(buf * 4096, 4096).bitcast(F32)
            outt = a_(8192 + buf * 4096, 4096).bitcast(F32)
            if tsub >= 2:
                kb.wait("sync", "scalar", norm4[buf])
                kb.wait("sync", "vector", addv4[buf])
            av1 = kb.dma("sync", acct, acc[tsub * 128:(tsub + 1) * 128, :])
            xv1 = kb.dma("sync", x1t, x1sp[tsub * 128:(tsub + 1) * 128, :])
            kb.wait("scalar", "syncd", av1)
            s1, s2, mean, var, t0, inv, nmi = (ST[:, j:j + 1] for j in range(7))
            kb.op("scalar", "activation", junk4, acct, AFT.Identity, accum_out=s1)
            v_sc = kb.opi("scalar", "activation", junk4, acct, AFT.Square,
                          accum_out=s2)
            kb.wait("vector", "scalar", v_sc)
            kb.tiny("vector", "tensor_scalar", mean, s1, 1.0 / E, None, ALU.mult)
            kb.tiny("vector", "tensor_scalar", var, s2, 1.0 / E, None, ALU.mult)
            kb.tiny("vector", "tensor_tensor", t0, mean, mean, ALU.mult)
            v_ve = kb.tiny("vector", "tensor_sub", var, var, t0)
            kb.wait("scalar", "vector", v_ve)
            kb.tiny("scalar", "activation", t0, var, AFT.Ln, bias=epscol[:])
            v_inv = kb.tiny("scalar", "activation", inv, t0, AFT.Exp, scale=-0.5)
            kb.wait("vector", "scalar", v_inv)
            kb.tiny("vector", "tensor_scalar", nmi, mean, -1.0, None, ALU.mult)
            v_nmi = kb.tiny("vector", "tensor_tensor", nmi, nmi, inv, ALU.mult)
            kb.wait("scalar", "vector", v_nmi)
            norm4[buf] = kb.opi("scalar", "activation", acct, acct,
                                AFT.Identity, bias=nmi, scale=inv)
            kb.wait("vector", "scalar", norm4[buf])
            kb.wait("vector", "syncd", xv1)
            if tsub >= 2:
                kb.wait("vector", "gpsimdd", out_done[buf])
            v_o = kb.opi("vector", "tensor_add", outt, acct, x1t)
            addv4[buf] = v_o
            kb.wait("gpsimd", "vector", v_o)
            out_done[buf] = kb.dma("gpsimd", out[tsub * 128:(tsub + 1) * 128, :],
                                   outt)
        kb.barrier()

    if PHASES < 5:
        v_o = 0
        for i in range(TQ // 128):
            v_o = kb.dma("sync", out[i * 128:(i + 1) * 128, :],
                         Fa[:, 0:2048])
        kb.wait("sync", "syncd", v_o)

    return kb


def host_prepare(L0_attn_x, L0_attn_Wq, L0_attn_bq, L0_attn_Wk, L0_attn_bk,
                 L0_attn_Wv, L0_attn_bv, L0_attn_Wo, L0_attn_bo,
                 L0_ffn_gate_W_packed, L0_ffn_gate_scales, L0_ffn_up_W_packed,
                 L0_ffn_up_scales, L0_ffn_down_W_packed, L0_ffn_down_scales):
    x = np.ascontiguousarray(np.asarray(L0_attn_x, dtype=np.float32))

    def wt(w):
        return np.ascontiguousarray(np.asarray(w, np.float32).T)

    def pk(p):
        a = np.ascontiguousarray(np.asarray(p, np.int32)).view(np.uint8)[:, 0::4]
        a = np.ascontiguousarray(a).view(np.uint16)
        return np.ascontiguousarray(a.T)

    def sc(s):
        return np.ascontiguousarray(np.asarray(s, np.float32).T)

    shared = {
        "wqT": wt(L0_attn_Wq), "wkT": wt(L0_attn_Wk),
        "wvT": wt(L0_attn_Wv), "woT": wt(L0_attn_Wo),
        "bq": np.asarray(L0_attn_bq, np.float32).reshape(1, E),
        "bk": np.asarray(L0_attn_bk, np.float32).reshape(1, E),
        "bv": np.asarray(L0_attn_bv, np.float32).reshape(1, E),
        "bo": np.asarray(L0_attn_bo, np.float32).reshape(1, E),
        "gpT": pk(L0_ffn_gate_W_packed), "upT": pk(L0_ffn_up_W_packed),
        "dpT": pk(L0_ffn_down_W_packed),
        "sgT": sc(L0_ffn_gate_scales), "suT": sc(L0_ffn_up_scales),
        "sdT": sc(L0_ffn_down_scales),
        "onesb": np.ones((1, 512), np.float32).astype(BFNP),
    }
    inv_freq = 1.0 / 10000.0 ** (np.arange(0, D, 2, dtype=np.float32) / D)
    pos = np.arange(S, dtype=np.float32)
    freqs = np.outer(pos, inv_freq)
    emb = np.concatenate([freqs, freqs], axis=-1)
    cos = np.cos(emb).astype(np.float32)
    sin = np.sin(emb).astype(np.float32)
    sinp = sin.copy()
    sinp[:, :64] *= -1.0
    cos_fm = np.ascontiguousarray(cos.T)
    sinp_fm = np.ascontiguousarray(sinp.T)
    shared["cos_kv"] = cos_fm
    shared["sinp_kv"] = sinp_fm
    hs = np.zeros((128, 256), np.float32)
    for h in range(16):
        hs[:, 16 * h + h] = 1.0
    shared["hsel"] = hs.astype(BFNP)

    in_maps = []
    for c in range(8):
        b = c // 2
        q0 = TQ * (c % 2)
        mcore = dict(shared)
        mcore["x"] = np.ascontiguousarray(x[b])
        mcore["xq"] = np.ascontiguousarray(x[b, q0:q0 + TQ])
        mcore["cos_q"] = np.ascontiguousarray(cos_fm[:, q0:q0 + TQ])
        mcore["sinp_q"] = np.ascontiguousarray(sinp_fm[:, q0:q0 + TQ])
        in_maps.append(mcore)
    return in_maps


def make_nc(dbg=()):
    nc = bass.Bass("TRN2", target_bir_lowering=False)
    kb = build_core_program(nc, dbg=dbg)
    with nc.Block() as block:
        for eng in ENGS:
            def body(e, eng=eng):
                marks = list(kb.marks[eng]) + [(len(kb.ops[eng]), None)]
                mi = 0
                cur = None
                import contextlib
                ctx = contextlib.nullcontext()
                for i, fn in enumerate(kb.ops[eng]):
                    while mi < len(marks) - 1 and i >= marks[mi][0]:
                        if cur is not None:
                            ctx.__exit__(None, None, None)
                            cur = None
                        cur = marks[mi][1]
                        ctx = nc.named_scope(cur)
                        ctx.__enter__()
                        mi += 1
                    fn(e)
                if cur is not None:
                    ctx.__exit__(None, None, None)
            getattr(block, eng)(body)
    return nc


def kernel(**inputs):
    in_maps = host_prepare(**inputs)
    nc = make_nc()
    res = run_bass_kernel_spmd(nc, in_maps, core_ids=list(range(8)))
    outp = np.zeros((B, S, E), np.float32)
    for c in range(8):
        b, q0 = c // 2, TQ * (c % 2)
        outp[b, q0:q0 + TQ] = res.results[c]["out"]
    return outp
